# version 01
# baseline (speedup 1.0000x reference)
import numpy as np
from contextlib import ExitStack
import concourse.bass as bass
import concourse.mybir as mybir
from concourse.bass_utils import run_bass_kernel_spmd

F32 = mybir.dt.float32
BF16 = mybir.dt.bfloat16
I32 = mybir.dt.int32
U32 = mybir.dt.uint32
AF = mybir.ActivationFunctionType
ALU = mybir.AluOpType
AX = mybir.AxisListType

TPC = 2048
NCORES = 8
DM = 2048
DIN = 4096
NST = 16
DTR = 128
GH = 5504
MOE_H = 2816
MOE_E = 8
MOE_C = 2048
MOE_T = NCORES * TPC


class S:

    def __init__(self, handle):
        self.h = handle
        self.n = 0

    def bump(self, k=1):
        self.n += k
        return self.n

    def total(self):
        return (self, self.n)


class Res:

    def __init__(self):
        self.deps = []

    def set(self, *deps):
        self.deps = [d for d in deps if d is not None]

    def waits(self):
        return list(self.deps)


class B:

    def __init__(self, nc, st):
        self.nc = nc
        self.st = st
        self.steps = []
        self._semid = 0

    def sem(self, name=None):
        self._semid += 1
        return S(self.st.enter_context(self.nc.semaphore(name or f"s{self._semid}")))

    def sems(self, k, name=None):
        return [self.sem(f"{name or 'q'}{i}") for i in range(k)]

    def op(self, eng_name, emit_fn, inc=None, inc_k=1, waits=()):
        flat = []
        for w in waits:
            if w is None:
                continue
            if isinstance(w, Res):
                flat.extend(w.waits())
            else:
                flat.append(w)
        best = {}
        for (s, v) in flat:
            if v and v > best.get(id(s), (s, 0))[1]:
                best[id(s)] = (s, v)
        vals = tuple(best.values())
        post = inc.bump(inc_k) if inc is not None else None

        def run(eng):
            for (s, v) in vals:
                eng.wait_ge(s.h, v)
            inst = emit_fn(eng)
            if inc is not None:
                inst.then_inc(inc.h, inc_k)

        self.steps.append((eng_name, run))
        return (inc, post) if inc is not None else None

    def finalize(self, block):
        used = []
        for e in ("sync", "tensor", "scalar", "vector", "gpsimd"):
            if any(s[0] == e for s in self.steps):
                used.append(e)

        def make(eng_name):
            def emit(eng):
                for (e, run) in self.steps:
                    if e == eng_name:
                        run(eng)
            return emit

        for e in used:
            getattr(block, e)(make(e))


def setup_consts(nc, csts, b, sPOOL, values):
    deps = []
    for i, val in enumerate(values):
        deps.append(b.op("gpsimd", lambda g, i=i, val=val: g.memset(csts[:, i:i + 1], val), inc=sPOOL))
        nc.const_aps.register(F32, val, csts[:, i:i + 1])
    return deps


def col1(dram_ap_1d):
    return dram_ap_1d.rearrange("(p one) -> p one", one=1)



DT_STAGE = 9


def build_p1(upto="full", nt_dt=None):
    nc = bass.Bass("TRN2", target_bir_lowering=False)
    T = TPC
    TE = T + 3
    NC = 512
    NCH = T // NC
    NT = DIN // 128

    xT = nc.dram_tensor("xT", [DM, TE], F32, kind="ExternalInput").ap()
    w_in = nc.dram_tensor("w_in", [DM, 2 * DIN], F32, kind="ExternalInput").ap()
    conv_w = nc.dram_tensor("conv_w", [DIN, 4], F32, kind="ExternalInput").ap()
    conv_b = nc.dram_tensor("conv_b", [DIN], F32, kind="ExternalInput").ap()
    w_xp = nc.dram_tensor("w_xp", [DIN, DTR + 2 * NST], F32, kind="ExternalInput").ap()
    w_dt = nc.dram_tensor("w_dt", [DTR, DIN], F32, kind="ExternalInput").ap()
    b_dt = nc.dram_tensor("b_dt", [DIN], F32, kind="ExternalInput").ap()

    hs_o = nc.dram_tensor("hs", [DIN, T], BF16, kind="ExternalOutput").ap()
    gate_o = nc.dram_tensor("gate", [DIN, T], BF16, kind="ExternalOutput").ap()
    dt_o = nc.dram_tensor("dt", [DIN, T], F32, kind="ExternalOutput").ap()
    dtu_o = nc.dram_tensor("dtu", [DIN, T], BF16, kind="ExternalOutput").ap()
    hs_i = nc.dram_tensor("hs_i", [DIN, T], BF16, kind="Internal").ap()
    bn_o = nc.dram_tensor("bn", [NST, T], F32, kind="ExternalOutput").ap()
    cn_o = nc.dram_tensor("cn", [NST, T], F32, kind="ExternalOutput").ap()

    st = ExitStack()
    stA = ExitStack()
    with st:
        csts_sb = st.enter_context(nc.sbuf_tensor("csts_sb", [128, 8], F32))
        sp0_sb = st.enter_context(nc.sbuf_tensor("sp0_sb", [128, T], F32))
        sp1_sb = st.enter_context(nc.sbuf_tensor("sp1_sb", [32, T], F32))
        ones1_sb = st.enter_context(nc.sbuf_tensor("ones1_sb", [1, 128], F32))
        ones128_sb = st.enter_context(nc.sbuf_tensor("ones128_sb", [128, 1], F32))
        sel_sb = st.enter_context(nc.sbuf_tensor("sel_sb", [32, 2], F32))
        selbc_sb = st.enter_context(nc.sbuf_tensor("selbc_sb", [2, 32], F32))
        io32r_sb = st.enter_context(nc.sbuf_tensor("io32r_sb", [32, 2], F32))
        io32c_sb = st.enter_context(nc.sbuf_tensor("io32c_sb", [32, 1], F32))
        io2r_sb = st.enter_context(nc.sbuf_tensor("io2r_sb", [2, 32], F32))
        io2c_sb = st.enter_context(nc.sbuf_tensor("io2c_sb", [2, 1], F32))
        zz_sb = st.enter_context(nc.sbuf_tensor("zz_sb", [32, 4], F32))
        x_sb = stA.enter_context(nc.sbuf_tensor("x_sb", [128, 16 * TE], BF16))
        wh_sb = stA.enter_context(nc.sbuf_tensor("wh_sb", [128, 2 * 2048], BF16))
        wg_sb = stA.enter_context(nc.sbuf_tensor("wg_sb", [128, 2 * 2048], BF16))
        wxp_sb = stA.enter_context(nc.sbuf_tensor("wxp_sb", [128, 2 * 160], BF16))
        cw_sb = stA.enter_context(nc.sbuf_tensor("cw_sb", [128, 2 * 4], F32))
        cb_sb = stA.enter_context(nc.sbuf_tensor("cb_sb", [128, 2], F32))
        hid_sb = stA.enter_context(nc.sbuf_tensor("hid_sb", [128, 2 * TE], F32))
        cv_sb = stA.enter_context(nc.sbuf_tensor("cv_sb", [128, 2 * T], F32))
        hsb_sb = stA.enter_context(nc.sbuf_tensor("hsb_sb", [128, 2 * T], BF16))
        gb_sb = stA.enter_context(nc.sbuf_tensor("gb_sb", [128, 2 * T], BF16))
        pH = st.enter_context(nc.psum_tensor("pH", [128, NC], F32))
        pG = st.enter_context(nc.psum_tensor("pG", [128, NC], F32))
        pX = st.enter_context(nc.psum_tensor("pX", [128, NC], F32))
        pX2 = st.enter_context(nc.psum_tensor("pX2", [32, NC], F32))
        pB = st.enter_context(nc.psum_tensor("pB", [128, 8], F32))
        pR = st.enter_context(nc.psum_tensor("pR", [128, NC], F32))

        b = B(nc, st)
        sPE, sACT, sDVE, sPOOL = b.sem("sPE"), b.sem("sACT"), b.sem("sDVE"), b.sem("sPOOL")
        sX = b.sem("sX")
        sWH = b.sems(2, "sWH"); sWG = b.sems(2, "sWG"); sWXP = b.sems(2, "sWXP")
        sCW = b.sems(2, "sCW"); sCB = b.sems(2, "sCB"); sDB = b.sems(2, "sDB")
        sWDT = b.sems(2, "sWDT"); sHSF = b.sems(2, "sHSF")
        sGST = b.sems(2, "sGST"); sHST = b.sems(2, "sHST")
        sDST = b.sems(2, "sDST"); sUST = b.sems(2, "sUST")
        sMISC = b.sem("sMISC")
        block = st.enter_context(nc.Block())

        _cdeps = setup_consts(nc, csts_sb, b, sPOOL, [1e-6, 1.0 / DTR, 1.0 / NST, -0.5, 1e-5, 1.0 / DM])

        r_pH = Res(); r_pG = Res(); r_pX = Res(); r_pX2 = Res(); r_pB = Res(); r_pR = Res()
        r_wh = [Res(), Res()]; r_wg = [Res(), Res()]; r_wxp = [Res(), Res()]
        r_cw = [Res(), Res()]; r_cb = [Res(), Res()]; r_db = [Res(), Res()]
        r_wdt = [Res(), Res()]; r_hsf = [Res(), Res()]
        r_hid = [Res(), Res()]; r_cv = [Res(), Res()]; r_hsb = [Res(), Res()]; r_gb = [Res(), Res()]
        r_dt = [Res(), Res()]; r_dtu = [Res(), Res()]
        r_z1 = [Res(), Res()]; r_z2 = [Res(), Res()]

        v_ones1 = b.op("gpsimd", lambda g: g.memset(ones1_sb[:, :], 1.0), inc=sPOOL)
        v_ones128 = b.op("gpsimd", lambda g: g.memset(ones128_sb[:, :], 1.0), inc=sPOOL)
        v_io32r = b.op("gpsimd", lambda g: g.iota(io32r_sb[:, :], pattern=[[1, 2]], channel_multiplier=0, allow_small_or_imprecise_dtypes=True), inc=sPOOL)
        v_io32c = b.op("gpsimd", lambda g: g.iota(io32c_sb[:, :], pattern=[[0, 1]], channel_multiplier=1, allow_small_or_imprecise_dtypes=True), inc=sPOOL)
        v_io2r = b.op("gpsimd", lambda g: g.iota(io2r_sb[:, :], pattern=[[1, 2], [0, 16]], channel_multiplier=0, allow_small_or_imprecise_dtypes=True), inc=sPOOL)
        v_io2c = b.op("gpsimd", lambda g: g.iota(io2c_sb[:, :], pattern=[[0, 1]], channel_multiplier=1, allow_small_or_imprecise_dtypes=True), inc=sPOOL)
        v1 = b.op("vector", lambda v: v.tensor_scalar(zz_sb[:, 0:2], io32r_sb[:, :], 16.0, io32c_sb[:, 0:1], ALU.mult, ALU.subtract),
                  inc=sDVE, waits=[v_io32r, v_io32c])
        v2 = b.op("vector", lambda v: v.tensor_scalar(zz_sb[:, 2:4], zz_sb[:, 0:2], 0.0, None, ALU.is_le),
                  inc=sDVE, waits=[v1])
        v3 = b.op("vector", lambda v: v.tensor_scalar(zz_sb[:, 0:2], zz_sb[:, 0:2], -16.0, None, ALU.is_gt),
                  inc=sDVE, waits=[v2])
        v_sel = b.op("vector", lambda v: v.tensor_mul(sel_sb[:, :], zz_sb[:, 0:2], zz_sb[:, 2:4]),
                     inc=sDVE, waits=[v3])
        v_selbc = b.op("vector", lambda v: v.tensor_scalar(selbc_sb[:, :], io2r_sb[:, :], io2c_sb[:, 0:1], None, ALU.is_equal),
                       inc=sDVE, waits=[v_io2r, v_io2c])

        for k in range(16):
            b.op("gpsimd", lambda g, k=k: g.dma_start(out=x_sb[:, k * TE:(k + 1) * TE], in_=xT[k * 128:(k + 1) * 128, :]),
                 inc=sX, inc_k=16)
        v_xload = sX.total()

        v_sp0 = b.op("gpsimd", lambda g: g.memset(sp0_sb[:, :], 0.0), inc=sPOOL)
        v_sp1 = b.op("gpsimd", lambda g: g.memset(sp1_sb[:, :], 0.0), inc=sPOOL)

        v_xp_last = None
        for c in range(NT):
            par = c & 1
            c0 = c * 128
            for k in range(16):
                wlh = b.op("gpsimd", lambda g, c0=c0, par=par, k=k: g.dma_start(
                    out=wh_sb[:, par * 2048 + k * 128:par * 2048 + (k + 1) * 128],
                    in_=w_in[k * 128:(k + 1) * 128, c0:c0 + 128]),
                    inc=sWH[par], inc_k=16, waits=[r_wh[par]] if k == 0 else [])
            wlh = sWH[par].total()
            for k in range(16):
                wlg = b.op("gpsimd", lambda g, c0=c0, par=par, k=k: g.dma_start(
                    out=wg_sb[:, par * 2048 + k * 128:par * 2048 + (k + 1) * 128],
                    in_=w_in[k * 128:(k + 1) * 128, DIN + c0:DIN + c0 + 128]),
                    inc=sWG[par], inc_k=16, waits=[r_wg[par]] if k == 0 else [])
            wlg = sWG[par].total()
            wlx = b.op("gpsimd", lambda g, c0=c0, par=par: g.dma_start(
                out=wxp_sb[:, par * 160:(par + 1) * 160], in_=w_xp[c0:c0 + 128, :]),
                inc=sWXP[par], inc_k=16, waits=[r_wxp[par]])
            wlc = b.op("sync", lambda s, c0=c0, par=par: s.dma_start(
                out=cw_sb[:, par * 4:(par + 1) * 4], in_=conv_w[c0:c0 + 128, :]),
                inc=sCW[par], inc_k=16, waits=[r_cw[par]])
            wlb = b.op("sync", lambda s, c0=c0, par=par: s.dma_start(
                out=cb_sb[:, par:par + 1], in_=col1(conv_b[c0:c0 + 128])),
                inc=sCB[par], inc_k=16, waits=[r_cb[par]])

            hid = hid_sb[:, par * TE:(par + 1) * TE]
            gbuf = gb_sb[:, par * T:(par + 1) * T]
            hsb = hsb_sb[:, par * T:(par + 1) * T]
            cvb = cv_sb[:, par * T:(par + 1) * T]
            whp = wh_sb[:, par * 2048:(par + 1) * 2048]
            wgp = wg_sb[:, par * 2048:(par + 1) * 2048]
            wxpp = wxp_sb[:, par * 160:(par + 1) * 160]

            v_mm = None
            for k in range(16):
                v_mm = b.op("tensor", lambda t, k=k, whp=whp: t.matmul(
                    pB[:, 0:3], whp[:, k * 128:(k + 1) * 128], x_sb[:, k * TE:k * TE + 3],
                    start=(k == 0), stop=(k == 15)),
                    inc=sPE if k == 15 else None,
                    waits=([wlh, v_xload, r_pB] if k == 0 else []))
            v_bnd = b.op("scalar", lambda s, hid=hid: s.copy(out=hid[:, 0:3], in_=pB[:, 0:3]),
                         inc=sACT, waits=[v_mm, r_hid[par]])
            r_pB.set(v_bnd)
            hid_writes = [v_bnd]

            v_gev_list = []
            for j in range(NCH):
                t0 = 3 + j * NC
                v_mm = None
                for k in range(16):
                    v_mm = b.op("tensor", lambda t, k=k, t0=t0, whp=whp: t.matmul(
                        pH[:, :], whp[:, k * 128:(k + 1) * 128], x_sb[:, k * TE + t0:k * TE + t0 + NC],
                        start=(k == 0), stop=(k == 15)),
                        inc=sPE if k == 15 else None,
                        waits=([r_pH] if k == 0 else []))
                v_ev = b.op("scalar", lambda s, t0=t0, hid=hid: s.copy(out=hid[:, t0:t0 + NC], in_=pH[:, :]),
                            inc=sACT, waits=[v_mm] + ([r_hid[par]] if j == 0 else []))
                r_pH.set(v_ev)
                hid_writes.append(v_ev)
                v_gm = None
                for k in range(16):
                    v_gm = b.op("tensor", lambda t, k=k, t0=t0, wgp=wgp: t.matmul(
                        pG[:, :], wgp[:, k * 128:(k + 1) * 128], x_sb[:, k * TE + t0:k * TE + t0 + NC],
                        start=(k == 0), stop=(k == 15)),
                        inc=sPE if k == 15 else None,
                        waits=([wlg, r_pG] if k == 0 else []))
                v_gev = b.op("vector", lambda v, j=j, gbuf=gbuf: v.tensor_copy(gbuf[:, j * NC:(j + 1) * NC], pG[:, :]),
                             inc=sDVE, waits=[v_gm] + ([r_gb[par]] if j == 0 else []))
                r_pG.set(v_gev)
                v_gev_list.append(v_gev)
            last_hid_mm = v_mm
            last_gate_mm = v_gm
            v_gst = b.op("gpsimd", lambda g, c0=c0, gbuf=gbuf: g.dma_start(out=gate_o[c0:c0 + 128, :], in_=gbuf[:, :]),
                         inc=sGST[par], inc_k=16, waits=[v_gev_list[-1]])
            r_gb[par].set(v_gst)
            r_wh[par].set(last_hid_mm)
            r_wg[par].set(last_gate_mm)

            v_c = b.op("vector", lambda v, hid=hid, cvb=cvb, par=par: v.tensor_scalar(
                cvb[:, :], hid[:, 0:T], cw_sb[:, par * 4:par * 4 + 1], None, ALU.mult),
                inc=sDVE, waits=hid_writes + [wlc, r_cv[par]])
            for k in range(1, 4):
                v_c = b.op("vector", lambda v, k=k, hid=hid, cvb=cvb, par=par: v.scalar_tensor_tensor(
                    cvb[:, :], hid[:, k:k + T], cw_sb[:, par * 4 + k:par * 4 + k + 1], cvb[:, :], ALU.mult, ALU.add),
                    inc=sDVE, waits=[v_c])
            r_hid[par].set(v_c)
            r_cw[par].set(v_c)
            v_hs = b.op("scalar", lambda s, cvb=cvb, hsb=hsb, par=par: s.activation(
                out=hsb[:, :], in_=cvb[:, :], func=AF.Silu, bias=cb_sb[:, par:par + 1]),
                inc=sACT, waits=[v_c, wlb, r_hsb[par]])
            r_cv[par].set(v_hs)
            r_cb[par].set(v_hs)
            v_hst = b.op("gpsimd", lambda g, c0=c0, hsb=hsb: g.dma_start(out=hs_o[c0:c0 + 128, :], in_=hsb[:, :]),
                         inc=sHST[par], inc_k=16, waits=[v_hs])
            v_hsti = b.op("gpsimd", lambda g, c0=c0, hsb=hsb: g.dma_start(out=hs_i[c0:c0 + 128, :], in_=hsb[:, :]),
                          inc=sHST[par], inc_k=16, waits=[v_hs])

            for j in range(NCH):
                t0 = j * NC
                vx = b.op("tensor", lambda t, t0=t0, wxpp=wxpp, hsb=hsb: t.matmul(
                    pX[:, :], wxpp[:, 0:128], hsb[:, t0:t0 + NC], start=True, stop=True),
                    inc=sPE, waits=[v_hs, wlx, r_pX])
                v_a0 = b.op("vector", lambda v, t0=t0: v.tensor_add(sp0_sb[:, t0:t0 + NC], sp0_sb[:, t0:t0 + NC], pX[:, :]),
                            inc=sDVE, waits=[vx, v_sp0])
                r_pX.set(v_a0)
                vx2 = b.op("tensor", lambda t, t0=t0, wxpp=wxpp, hsb=hsb: t.matmul(
                    pX2[:, :], wxpp[:, 128:160], hsb[:, t0:t0 + NC], start=True, stop=True),
                    inc=sPE, waits=[r_pX2])
                v_a1 = b.op("vector", lambda v, t0=t0: v.tensor_add(sp1_sb[:, t0:t0 + NC], sp1_sb[:, t0:t0 + NC], pX2[:, :]),
                            inc=sDVE, waits=[vx2, v_sp1])
                r_pX2.set(v_a1)
                v_xp_last = (v_a1, vx2)
            r_wxp[par].set(v_xp_last[1])
            r_hsb[par].set(v_xp_last[1], v_hst, v_hsti)
        v_spdone = v_xp_last[0]
        hs_all_stored = [sHST[0].total(), sHST[1].total()]

        if upto == "A":
            _early = True
            b.op("gpsimd", lambda g: g.dma_start(out=dt_o[0:128, :], in_=sp0_sb[:, :]),
                 inc=sMISC, inc_k=16, waits=[v_spdone])
            b.op("gpsimd", lambda g: g.engine_nop(),
                 waits=[sMISC.total(), sGST[0].total(), sGST[1].total(), sHST[0].total(), sHST[1].total()])
            b.finalize(block)
            stA.close()
            return nc

        stA.close()
        BAR = [sX.total(), sWH[0].total(), sWH[1].total(), sWG[0].total(), sWG[1].total(),
               sWXP[0].total(), sWXP[1].total(), sCW[0].total(), sCW[1].total(),
               sCB[0].total(), sCB[1].total(), sGST[0].total(), sGST[1].total(),
               sHST[0].total(), sHST[1].total(),
               (sPE, sPE.n), (sACT, sACT.n), (sDVE, sDVE.n), (sPOOL, sPOOL.n)]
        sq_sb = st.enter_context(nc.sbuf_tensor("sq_sb", [128, T], F32))
        row_sb = st.enter_context(nc.sbuf_tensor("row_sb", [1, T], F32))
        rbc_sb = st.enter_context(nc.sbuf_tensor("rbc_sb", [128, T], F32))
        dtnf_sb = st.enter_context(nc.sbuf_tensor("dtnf_sb", [128, T], F32))
        dtn_sb = st.enter_context(nc.sbuf_tensor("dtn_sb", [128, T], BF16))
        rbc2_sb = st.enter_context(nc.sbuf_tensor("rbc2_sb", [2, T], F32))
        sq32_sb = st.enter_context(nc.sbuf_tensor("sq32_sb", [32, T], F32))
        bc32_sb = st.enter_context(nc.sbuf_tensor("bc32_sb", [32, T], F32))
        dt_sb = st.enter_context(nc.sbuf_tensor("dt_sb", [128, 2 * T], F32))
        hsf_sb = st.enter_context(nc.sbuf_tensor("hsf_sb", [128, 2 * T], F32))
        dtu_sb = st.enter_context(nc.sbuf_tensor("dtu_sb", [128, 2 * T], BF16))
        z1_sb = st.enter_context(nc.sbuf_tensor("z1_sb", [128, 2 * T], F32))
        z2_sb = st.enter_context(nc.sbuf_tensor("z2_sb", [128, 2 * T], F32))
        wdt_sb = st.enter_context(nc.sbuf_tensor("wdt_sb", [128, 2 * 128], BF16))
        db_sb = st.enter_context(nc.sbuf_tensor("db_sb", [128, 2], F32))

        v_sq = b.op("scalar", lambda s: s.activation(out=sq_sb[:, :], in_=sp0_sb[:, :], func=AF.Square),
                    inc=sACT, waits=[v_spdone] + BAR)
        v_rcp = None
        for j in range(NCH):
            v_rmm = b.op("tensor", lambda t, j=j: t.matmul(
                pR[0:1, :], ones128_sb[:, 0:1], sq_sb[:, j * NC:(j + 1) * NC], start=True, stop=True),
                inc=sPE, waits=[v_sq, v_ones1, v_ones128, r_pR])
            v_rcp = b.op("vector", lambda v, j=j: v.tensor_copy(row_sb[0:1, j * NC:(j + 1) * NC], pR[0:1, :]),
                         inc=sDVE, waits=[v_rmm] + BAR)
            r_pR.set(v_rcp)
        v_ln = b.op("scalar", lambda s: s.activation(out=row_sb[0:1, :], in_=row_sb[0:1, :], func=AF.Ln, scale=1.0 / DTR, bias=1e-6),
                    inc=sACT, waits=[v_rcp])
        v_ex = b.op("scalar", lambda s: s.activation(out=row_sb[0:1, :], in_=row_sb[0:1, :], func=AF.Exp, scale=-0.5),
                    inc=sACT, waits=[v_ln])
        v_bc = None
        for j in range(NCH):
            v_bcm = b.op("tensor", lambda t, j=j: t.matmul(
                pR[:, :], ones1_sb[0:1, :], row_sb[0:1, j * NC:(j + 1) * NC], start=True, stop=True),
                inc=sPE, waits=[v_ex, r_pR])
            v_bc = b.op("vector", lambda v, j=j: v.tensor_copy(rbc_sb[:, j * NC:(j + 1) * NC], pR[:, :]),
                        inc=sDVE, waits=[v_bcm] + BAR)
            r_pR.set(v_bc)
        v_dtnf = b.op("vector", lambda v: v.tensor_mul(dtnf_sb[:, :], sp0_sb[:, :], rbc_sb[:, :]),
                      inc=sDVE, waits=[v_bc] + BAR)
        v_dtn = b.op("vector", lambda v: v.tensor_copy(dtn_sb[:, :], dtnf_sb[:, :]),
                     inc=sDVE, waits=[v_dtnf] + BAR)

        v_sq2 = b.op("scalar", lambda s: s.activation(out=sq32_sb[:, :], in_=sp1_sb[:, :], func=AF.Square),
                     inc=sACT, waits=[v_spdone] + BAR)
        v_c2 = None
        for j in range(NCH):
            v_m2 = b.op("tensor", lambda t, j=j: t.matmul(
                pR[0:2, :], sel_sb[:, :], sq32_sb[:, j * NC:(j + 1) * NC], start=True, stop=True),
                inc=sPE, waits=[v_sq2, v_sel, r_pR])
            v_c2 = b.op("vector", lambda v, j=j: v.tensor_copy(rbc2_sb[:, j * NC:(j + 1) * NC], pR[0:2, :]),
                        inc=sDVE, waits=[v_m2] + BAR)
            r_pR.set(v_c2)
        v_ln2 = b.op("scalar", lambda s: s.activation(out=rbc2_sb[:, :], in_=rbc2_sb[:, :], func=AF.Ln, scale=1.0 / NST, bias=1e-6),
                     inc=sACT, waits=[v_c2])
        v_ex2 = b.op("scalar", lambda s: s.activation(out=rbc2_sb[:, :], in_=rbc2_sb[:, :], func=AF.Exp, scale=-0.5),
                     inc=sACT, waits=[v_ln2])
        v_c3 = None
        for j in range(NCH):
            v_m3 = b.op("tensor", lambda t, j=j: t.matmul(
                pX2[:, :], selbc_sb[:, :], rbc2_sb[:, j * NC:(j + 1) * NC], start=True, stop=True),
                inc=sPE, waits=[v_ex2, v_selbc, r_pX2])
            v_c3 = b.op("vector", lambda v, j=j: v.tensor_copy(bc32_sb[:, j * NC:(j + 1) * NC], pX2[:, :]),
                        inc=sDVE, waits=[v_m3] + BAR)
            r_pX2.set(v_c3)
        v_bc32 = b.op("vector", lambda v: v.tensor_mul(bc32_sb[:, :], sp1_sb[:, :], bc32_sb[:, :]),
                      inc=sDVE, waits=[v_c3])
        b.op("gpsimd", lambda g: g.dma_start(out=bn_o[:, :], in_=bc32_sb[0:16, :]),
             inc=sMISC, inc_k=16, waits=[v_bc32])
        b.op("gpsimd", lambda g: g.dma_start(out=cn_o[:, :], in_=bc32_sb[16:32, :]),
             inc=sMISC, inc_k=16, waits=[v_bc32])

        if upto == "B":
            b.op("gpsimd", lambda g: g.dma_start(out=dt_o[0:128, :], in_=dtnf_sb[:, :]),
                 inc=sMISC, inc_k=16, waits=[v_dtn])
            b.op("gpsimd", lambda g: g.engine_nop(),
                 waits=[sMISC.total(), sGST[0].total(), sGST[1].total(), sHST[0].total(), sHST[1].total()])
            b.finalize(block)
            stA.close()
            return nc

        for c in range(nt_dt if nt_dt is not None else NT):
            par = c & 1
            c0 = c * 128
            wl = b.op("gpsimd", lambda g, c0=c0, par=par: g.dma_start(
                out=wdt_sb[:, par * 128:(par + 1) * 128], in_=w_dt[:, c0:c0 + 128]),
                inc=sWDT[par], inc_k=16, waits=[r_wdt[par]] + BAR)
            bl = b.op("sync", lambda s, c0=c0, par=par: s.dma_start(
                out=db_sb[:, par:par + 1], in_=col1(b_dt[c0:c0 + 128])),
                inc=sDB[par], inc_k=16, waits=[r_db[par]] + BAR)
            hl = b.op("gpsimd", lambda g, c0=c0, par=par: g.dma_start(
                out=hsf_sb[:, par * T:(par + 1) * T], in_=hs_i[c0:c0 + 128, :]),
                inc=sHSF[par], inc_k=16, waits=hs_all_stored + [r_hsf[par]] + BAR)
            dtb = dt_sb[:, par * T:(par + 1) * T]
            hsf = hsf_sb[:, par * T:(par + 1) * T]
            dtub = dtu_sb[:, par * T:(par + 1) * T]
            z1 = z1_sb[:, par * T:(par + 1) * T]
            z2 = z2_sb[:, par * T:(par + 1) * T]
            v_adds = []
            v_mm_last = None
            for j in range(NCH):
                t0 = j * NC
                vm = b.op("tensor", lambda t, par=par, t0=t0: t.matmul(
                    pH[:, :], wdt_sb[:, par * 128:(par + 1) * 128], dtn_sb[:, t0:t0 + NC], start=True, stop=True),
                    inc=sPE, waits=[wl, v_dtn, r_pH])
                v_mm_last = vm
                va = b.op("scalar", lambda s, t0=t0, z1=z1, par=par: s.activation(
                    out=z1[:, t0:t0 + NC], in_=pH[:, :], func=AF.Abs, bias=db_sb[:, par:par + 1]),
                    inc=sACT, waits=[vm, bl] + (([r_z1[par]] + BAR) if j == 0 else []))
                vr = b.op("vector", lambda v, t0=t0, dtb=dtb, par=par: v.tensor_scalar(
                    dtb[:, t0:t0 + NC], pH[:, :], db_sb[:, par:par + 1], 0.0, ALU.add, ALU.max),
                    inc=sDVE, waits=[vm, bl, va] + ((BAR + [r_dt[par]]) if j == 0 else []))
                r_pH.set(va, vr)
                if DT_STAGE >= 3:
                    ve = b.op("scalar", lambda s, t0=t0, z1=z1, z2=z2: s.activation(
                        out=z2[:, t0:t0 + NC], in_=z1[:, t0:t0 + NC], func=AF.Exp, scale=-1.0),
                        inc=sACT, waits=[va] + ((BAR + [r_z2[par]]) if j == 0 else []))
                    vl = b.op("scalar", lambda s, t0=t0, z2=z2: s.activation(
                        out=z2[:, t0:t0 + NC], in_=z2[:, t0:t0 + NC], func=AF.Ln, bias=1.0),
                        inc=sACT, waits=[ve])
                else:
                    ve = vl = va
                if DT_STAGE >= 4:
                    vadd = b.op("vector", lambda v, t0=t0, dtb=dtb, z2=z2: v.tensor_add(
                        dtb[:, t0:t0 + NC], dtb[:, t0:t0 + NC], z2[:, t0:t0 + NC]),
                        inc=sDVE, waits=[vl, vr])
                else:
                    vadd = vr
                v_adds.append(vadd)
            r_wdt[par].set(v_mm_last)
            r_db[par].set(va, vr)
            r_z1[par].set(ve)
            r_z2[par].set(v_adds[-1])
            v_dst = b.op("gpsimd", lambda g, c0=c0, dtb=dtb: g.dma_start(out=dt_o[c0:c0 + 128, :], in_=dtb[:, :]),
                         inc=sDST[par], inc_k=16, waits=v_adds[-1:])
            if DT_STAGE >= 6:
                v_dtu = b.op("vector", lambda v, dtb=dtb, hsf=hsf, dtub=dtub: v.tensor_mul(dtub[:, :], dtb[:, :], hsf[:, :]),
                             inc=sDVE, waits=[hl] + v_adds + [r_dtu[par]] + BAR)
                r_hsf[par].set(v_dtu)
                r_dt[par].set(v_dtu, v_dst)
                v_ust = b.op("gpsimd", lambda g, c0=c0, dtub=dtub: g.dma_start(out=dtu_o[c0:c0 + 128, :], in_=dtub[:, :]),
                             inc=sUST[par], inc_k=16, waits=[v_dtu])
                r_dtu[par].set(v_ust)
            else:
                r_dt[par].set(v_dst)

        finals = [sMISC.total(), sGST[0].total(), sGST[1].total(), sHST[0].total(), sHST[1].total(),
                  sDST[0].total(), sDST[1].total(), sUST[0].total(), sUST[1].total()]
        b.op("gpsimd", lambda g: g.engine_nop(), waits=finals)
        b.finalize(block)
    return nc



def build_p2():
    nc = bass.Bass("TRN2", target_bir_lowering=False)
    CH = 512
    TT = NCORES * TPC
    NG = CH // 8
    CK = 2048
    NCK = TT // CK
    NC = 512
    NSUB = CK // NC

    dt_d = nc.dram_tensor("dt_s", [CH, TT], F32, kind="ExternalInput").ap()
    dtu_d = nc.dram_tensor("dtu_s", [CH, TT], BF16, kind="ExternalInput").ap()
    bn_d = nc.dram_tensor("bn_f", [NST, TT], F32, kind="ExternalInput").ap()
    cn_d = nc.dram_tensor("cn_f", [NST, TT], F32, kind="ExternalInput").ap()
    al_d = nc.dram_tensor("alog_s", [CH * NST], F32, kind="ExternalInput").ap()
    y_d = nc.dram_tensor("y_s", [CH, TT], F32, kind="ExternalOutput").ap()

    st = ExitStack()
    with st:
        csts_sb = st.enter_context(nc.sbuf_tensor("csts_sb", [128, 8], F32))
        rep01f = st.enter_context(nc.sbuf_tensor("rep01f", [8, 128], F32))
        rep01b = st.enter_context(nc.sbuf_tensor("rep01b", [8, 128], BF16))
        rep01tb = st.enter_context(nc.sbuf_tensor("rep01tb", [128, 8], BF16))
        sel16 = st.enter_context(nc.sbuf_tensor("sel16", [16, 128], F32))
        io8r = st.enter_context(nc.sbuf_tensor("io8r", [8, 128], F32))
        io8c = st.enter_context(nc.sbuf_tensor("io8c", [8, 1], F32))
        io128r = st.enter_context(nc.sbuf_tensor("io128r", [128, 8], F32))
        io128c = st.enter_context(nc.sbuf_tensor("io128c", [128, 1], F32))
        io16r = st.enter_context(nc.sbuf_tensor("io16r", [16, 128], F32))
        io16c = st.enter_context(nc.sbuf_tensor("io16c", [16, 1], F32))
        ztmp = st.enter_context(nc.sbuf_tensor("ztmp", [128, 16], F32))
        repa = st.enter_context(nc.sbuf_tensor("repa", [8, NG * 128], F32))
        abc = st.enter_context(nc.sbuf_tensor("abc", [8, 2 * 128], F32))
        dt8 = st.enter_context(nc.sbuf_tensor("dt8", [8, 2 * CK], F32))
        dtu8 = st.enter_context(nc.sbuf_tensor("dtu8", [8, 2 * CK], BF16))
        bn_sb = st.enter_context(nc.sbuf_tensor("bn_sb", [16, 2 * CK], F32))
        cn_sb = st.enter_context(nc.sbuf_tensor("cn_sb", [16, 2 * CK], F32))
        brep = st.enter_context(nc.sbuf_tensor("brep", [128, CK], F32))
        crep = st.enter_context(nc.sbuf_tensor("crep", [128, CK], F32))
        da_sb = st.enter_context(nc.sbuf_tensor("da_sb", [128, CK], F32))
        dbu_sb = st.enter_context(nc.sbuf_tensor("dbu_sb", [128, CK], F32))
        s_sb = st.enter_context(nc.sbuf_tensor("s_sb", [128, CK], F32))
        prod_sb = st.enter_context(nc.sbuf_tensor("prod_sb", [128, CK], BF16))
        y_sb = st.enter_context(nc.sbuf_tensor("y_sb", [8, 2 * CK], F32))
        carry = st.enter_context(nc.sbuf_tensor("carry", [128, NG], F32))

        pA = st.enter_context(nc.psum_tensor("pA", [128, CK], F32))
        pU = st.enter_context(nc.psum_tensor("pU", [128, CK], F32))

        b = B(nc, st)
        sPE, sACT, sDVE, sPOOL = b.sem("sPE"), b.sem("sACT"), b.sem("sDVE"), b.sem("sPOOL")
        sAL = b.sems(2, "sAL")
        sDT8 = b.sems(2, "sDT8"); sDTU8 = b.sems(2, "sDTU8")
        sBN = b.sems(2, "sBN"); sCN = b.sems(2, "sCN")
        sYST = b.sems(2, "sYST")
        block = st.enter_context(nc.Block())

        r_pA = Res(); r_pU = Res(); r_pY = Res()
        r_al = [Res(), Res()]
        r_dt8 = [Res(), Res()]; r_dtu8 = [Res(), Res()]
        r_bn = [Res(), Res()]; r_cn = [Res(), Res()]
        r_brep = Res(); r_crep = Res()
        r_da = Res(); r_dbu = Res(); r_s = Res(); r_prod = Res()
        r_y = [Res(), Res()]

        v_i8r = b.op("gpsimd", lambda g: g.iota(io8r[:, :], pattern=[[1, 8], [0, 16]], channel_multiplier=0, allow_small_or_imprecise_dtypes=True), inc=sPOOL)
        v_i8c = b.op("gpsimd", lambda g: g.iota(io8c[:, :], pattern=[[0, 1]], channel_multiplier=1, allow_small_or_imprecise_dtypes=True), inc=sPOOL)
        v_i128r = b.op("gpsimd", lambda g: g.iota(io128r[:, :], pattern=[[1, 8]], channel_multiplier=0, allow_small_or_imprecise_dtypes=True), inc=sPOOL)
        v_i128c = b.op("gpsimd", lambda g: g.iota(io128c[:, :], pattern=[[0, 1]], channel_multiplier=1, allow_small_or_imprecise_dtypes=True), inc=sPOOL)
        v_i16r = b.op("gpsimd", lambda g: g.iota(io16r[:, :], pattern=[[0, 8], [1, 16]], channel_multiplier=0, allow_small_or_imprecise_dtypes=True), inc=sPOOL)
        v_i16c = b.op("gpsimd", lambda g: g.iota(io16c[:, :], pattern=[[0, 1]], channel_multiplier=1, allow_small_or_imprecise_dtypes=True), inc=sPOOL)
        v_r01 = b.op("vector", lambda v: v.tensor_scalar(rep01f[:, :], io8r[:, :], io8c[:, 0:1], None, ALU.is_equal),
                     inc=sDVE, waits=[v_i8r, v_i8c])
        v_r01b = b.op("vector", lambda v: v.tensor_copy(rep01b[:, :], rep01f[:, :]), inc=sDVE, waits=[v_r01])
        v_t1 = b.op("vector", lambda v: v.tensor_scalar(ztmp[:, 0:8], io128r[:, :], 16.0, io128c[:, 0:1], ALU.mult, ALU.subtract),
                    inc=sDVE, waits=[v_i128r, v_i128c])
        v_t2 = b.op("vector", lambda v: v.tensor_scalar(ztmp[:, 8:16], ztmp[:, 0:8], 0.0, None, ALU.is_le),
                    inc=sDVE, waits=[v_t1])
        v_t3 = b.op("vector", lambda v: v.tensor_scalar(ztmp[:, 0:8], ztmp[:, 0:8], -16.0, None, ALU.is_gt),
                    inc=sDVE, waits=[v_t2])
        v_r01t = b.op("vector", lambda v: v.tensor_mul(rep01tb[:, :], ztmp[:, 0:8], ztmp[:, 8:16]),
                      inc=sDVE, waits=[v_t3])
        v_s16 = b.op("vector", lambda v: v.tensor_scalar(sel16[:, :], io16r[:, :], io16c[:, 0:1], None, ALU.is_equal),
                     inc=sDVE, waits=[v_i16r, v_i16c])

        for g in range(NG):
            par = g & 1
            al = b.op("gpsimd", lambda gg, g=g, par=par: gg.dma_start(
                out=abc[:, par * 128:(par + 1) * 128],
                in_=al_d[g * 128:(g + 1) * 128].unsqueeze(0).partition_broadcast(8)),
                inc=sAL[par], inc_k=16, waits=[r_al[par]])
            ve = b.op("scalar", lambda s, par=par: s.activation(
                out=abc[:, par * 128:(par + 1) * 128], in_=abc[:, par * 128:(par + 1) * 128], func=AF.Exp),
                inc=sACT, waits=[al])
            vm = b.op("vector", lambda v, g=g, par=par: v.scalar_tensor_tensor(
                repa[:, g * 128:(g + 1) * 128], abc[:, par * 128:(par + 1) * 128], -1.0, rep01f[:, :], ALU.mult, ALU.mult),
                inc=sDVE, waits=[ve, v_r01])
            r_al[par].set(vm)
        v_repa = vm

        for ck in range(NCK):
            ckpar = ck & 1
            t0 = ck * CK
            bl = b.op("sync", lambda s, t0=t0, ckpar=ckpar: s.dma_start(
                out=bn_sb[:, ckpar * CK:(ckpar + 1) * CK], in_=bn_d[:, t0:t0 + CK]),
                inc=sBN[ckpar], inc_k=16, waits=[r_bn[ckpar]])
            cl = b.op("sync", lambda s, t0=t0, ckpar=ckpar: s.dma_start(
                out=cn_sb[:, ckpar * CK:(ckpar + 1) * CK], in_=cn_d[:, t0:t0 + CK]),
                inc=sCN[ckpar], inc_k=16, waits=[r_cn[ckpar]])
            v_br = None
            for sub in range(NSUB):
                vmB = b.op("tensor", lambda t, sub=sub, ckpar=ckpar: t.matmul(
                    pA[:, sub * NC:(sub + 1) * NC], sel16[:, :], bn_sb[:, ckpar * CK + sub * NC:ckpar * CK + (sub + 1) * NC],
                    start=True, stop=True),
                    inc=sPE, waits=[bl, v_s16, r_pA])
            v_brc = b.op("vector", lambda v: v.tensor_copy(brep[:, :], pA[:, :]), inc=sDVE, waits=[vmB, r_brep])
            r_pA.set(v_brc)
            for sub in range(NSUB):
                vmC = b.op("tensor", lambda t, sub=sub, ckpar=ckpar: t.matmul(
                    pA[:, sub * NC:(sub + 1) * NC], sel16[:, :], cn_sb[:, ckpar * CK + sub * NC:ckpar * CK + (sub + 1) * NC],
                    start=True, stop=True),
                    inc=sPE, waits=[cl, v_s16, r_pA])
            v_crc = b.op("vector", lambda v: v.tensor_copy(crep[:, :], pA[:, :]), inc=sDVE, waits=[vmC, r_crep])
            r_pA.set(v_crc)
            r_bn[ckpar].set(vmB)
            r_cn[ckpar].set(vmC)
            b_users = []
            c_users = []

            for g in range(NG):
                gpar = g & 1
                dl = b.op("gpsimd", lambda gg, g=g, t0=t0, gpar=gpar: gg.dma_start(
                    out=dt8[:, gpar * CK:(gpar + 1) * CK], in_=dt_d[g * 8:(g + 1) * 8, t0:t0 + CK]),
                    inc=sDT8[gpar], inc_k=16, waits=[r_dt8[gpar]])
                ul = b.op("gpsimd", lambda gg, g=g, t0=t0, gpar=gpar: gg.dma_start(
                    out=dtu8[:, gpar * CK:(gpar + 1) * CK], in_=dtu_d[g * 8:(g + 1) * 8, t0:t0 + CK]),
                    inc=sDTU8[gpar], inc_k=16, waits=[r_dtu8[gpar]])
                v_am = None
                for sub in range(NSUB):
                    v_am = b.op("tensor", lambda t, g=g, sub=sub, gpar=gpar: t.matmul(
                        pA[:, sub * NC:(sub + 1) * NC], repa[:, g * 128:(g + 1) * 128],
                        dt8[:, gpar * CK + sub * NC:gpar * CK + (sub + 1) * NC], start=True, stop=True),
                        inc=sPE, waits=([dl, v_repa, r_pA] if sub == 0 else []))
                v_da = b.op("scalar", lambda s: s.activation(out=da_sb[:, :], in_=pA[:, :], func=AF.Exp),
                            inc=sACT, waits=[v_am, r_da])
                r_pA.set(v_da)
                v_um = None
                for sub in range(NSUB):
                    v_um = b.op("tensor", lambda t, g=g, sub=sub, gpar=gpar: t.matmul(
                        pU[:, sub * NC:(sub + 1) * NC], rep01b[:, :],
                        dtu8[:, gpar * CK + sub * NC:gpar * CK + (sub + 1) * NC], start=True, stop=True),
                        inc=sPE, waits=([ul, v_r01b, r_pU] if sub == 0 else []))
                v_db = b.op("vector", lambda v: v.tensor_mul(dbu_sb[:, :], pU[:, :], brep[:, :]),
                            inc=sDVE, waits=[v_um, v_brc, r_dbu])
                r_pU.set(v_db)
                b_users.append(v_db)
                if ck % 4 == 0:
                    v_sc = b.op("vector", lambda v: v.tensor_tensor_scan(
                        s_sb[:, :], da_sb[:, :], dbu_sb[:, :], 0.0, ALU.mult, ALU.add),
                        inc=sDVE, waits=[v_da, v_db, r_s])
                else:
                    v_sc = b.op("vector", lambda v, g=g: v.tensor_tensor_scan(
                        s_sb[:, :], da_sb[:, :], dbu_sb[:, :], carry[:, g:g + 1], ALU.mult, ALU.add),
                        inc=sDVE, waits=[v_da, v_db, r_s])
                r_da.set(v_sc)
                r_dbu.set(v_sc)
                v_cw = b.op("vector", lambda v, g=g: v.tensor_copy(carry[:, g:g + 1], s_sb[:, CK - 1:CK]),
                            inc=sDVE, waits=[v_sc])
                v_pr = b.op("vector", lambda v: v.tensor_mul(prod_sb[:, :], s_sb[:, :], crep[:, :]),
                            inc=sDVE, waits=[v_sc, v_crc, r_prod])
                r_s.set(v_pr, v_cw)
                c_users.append(v_pr)
                v_ym = None
                for sub in range(NSUB):
                    v_ym = b.op("tensor", lambda t, sub=sub: t.matmul(
                        pA[0:8, sub * NC:(sub + 1) * NC], rep01tb[:, :],
                        prod_sb[:, sub * NC:(sub + 1) * NC], start=True, stop=True),
                        inc=sPE, waits=([v_pr, v_r01t, r_pA] if sub == 0 else []))
                v_ye = b.op("vector", lambda v, gpar=gpar: v.tensor_copy(y_sb[:, gpar * CK:(gpar + 1) * CK], pA[0:8, :]),
                            inc=sDVE, waits=[v_ym, r_y[gpar]])
                r_pA.set(v_ye)
                r_prod.set(v_ym)
                v_yst = b.op("gpsimd", lambda gg, g=g, t0=t0, gpar=gpar: gg.dma_start(
                    out=y_d[g * 8:(g + 1) * 8, t0:t0 + CK], in_=y_sb[:, gpar * CK:(gpar + 1) * CK]),
                    inc=sYST[gpar], inc_k=16, waits=[v_ye])
                r_y[gpar].set(v_yst)
                r_dt8[gpar].set(v_am)
                r_dtu8[gpar].set(v_um)
            r_brep.set(*b_users)
            r_crep.set(*c_users)

        b.op("gpsimd", lambda g: g.engine_nop(), waits=[sYST[0].total(), sYST[1].total()])
        b.finalize(block)
    return nc



def build_p3():
    nc = bass.Bass("TRN2", target_bir_lowering=False)
    T = TPC
    NC = 512
    NCH = T // NC
    NT = DIN // 128
    NM = DM // 128

    y_d = nc.dram_tensor("y_t", [DIN, T], F32, kind="ExternalInput").ap()
    hs_d = nc.dram_tensor("hs", [DIN, T], BF16, kind="ExternalInput").ap()
    gt_d = nc.dram_tensor("gate", [DIN, T], BF16, kind="ExternalInput").ap()
    x_d = nc.dram_tensor("xT", [DM, T], F32, kind="ExternalInput").ap()
    wo_d = nc.dram_tensor("w_out", [DIN, DM], F32, kind="ExternalInput").ap()
    dp_d = nc.dram_tensor("d_par", [DIN], F32, kind="ExternalInput").ap()
    x1_d = nc.dram_tensor("x1", [DM, T], BF16, kind="ExternalOutput").ap()
    xa_i = nc.dram_tensor("xa_i", [DM, T], F32, kind="Internal").ap()

    st = ExitStack()
    st1 = ExitStack()
    st2 = ExitStack()
    with st:
        csts_sb = st.enter_context(nc.sbuf_tensor("csts_sb", [128, 8], F32))
        ones1_sb = st.enter_context(nc.sbuf_tensor("ones1_sb", [1, 128], F32))
        ones128_sb = st.enter_context(nc.sbuf_tensor("ones128_sb", [128, 1], F32))
        sum_sb = st.enter_context(nc.sbuf_tensor("sum_sb", [1, T], F32))
        ssq_sb = st.enter_context(nc.sbuf_tensor("ssq_sb", [1, T], F32))
        tmp_sb = st.enter_context(nc.sbuf_tensor("tmp_sb", [1, T], F32))
        yo_sb = st.enter_context(nc.sbuf_tensor("yo_sb", [128, NT * T], BF16))
        yt_sb = st1.enter_context(nc.sbuf_tensor("yt_sb", [128, 2 * T], F32))
        hsf_sb = st1.enter_context(nc.sbuf_tensor("hsf_sb", [128, 2 * T], F32))
        gt_sb = st1.enter_context(nc.sbuf_tensor("gt_sb", [128, 2 * T], BF16))
        sg_sb = st1.enter_context(nc.sbuf_tensor("sg_sb", [128, T], F32))
        dp_sb = st1.enter_context(nc.sbuf_tensor("dp_sb", [128, 2], F32))

        pM = [st.enter_context(nc.psum_tensor(f"pM{i}", [128, T], F32)) for i in range(2)]

        b = B(nc, st)
        sPE, sACT, sDVE, sPOOL = b.sem("sPE"), b.sem("sACT"), b.sem("sDVE"), b.sem("sPOOL")
        sY = b.sems(2, "sY"); sHS = b.sems(2, "sHS"); sGT = b.sems(2, "sGT"); sDP = b.sems(2, "sDP")
        sWO = b.sems(2, "sWO"); sXT = b.sems(2, "sXT"); sXA = b.sems(2, "sXA")
        sXAL = b.sems(2, "sXAL"); sX1 = b.sems(2, "sX1")
        block = st.enter_context(nc.Block())

        _cdeps = setup_consts(nc, csts_sb, b, sPOOL, [1e-5, 1.0 / DM, -0.5])
        v_ones1 = b.op("gpsimd", lambda g: g.memset(ones1_sb[:, :], 1.0), inc=sPOOL)
        v_ones128 = b.op("gpsimd", lambda g: g.memset(ones128_sb[:, :], 1.0), inc=sPOOL)
        v_sum0 = b.op("gpsimd", lambda g: g.memset(sum_sb[:, :], 0.0), inc=sPOOL)
        v_ssq0 = b.op("gpsimd", lambda g: g.memset(ssq_sb[:, :], 0.0), inc=sPOOL)

        r_y = [Res(), Res()]; r_hs = [Res(), Res()]; r_gt = [Res(), Res()]; r_dp = [Res(), Res()]
        r_sg = Res(); r_yt = [Res(), Res()]
        r_wo = [Res(), Res()]; r_xt = [Res(), Res()]; r_xa = [Res(), Res()]; r_sq = [Res(), Res()]
        r_xab = [Res(), Res()]; r_x1 = [Res(), Res()]
        r_pM = [Res(), Res()]

        v_yo = None
        for c in range(NT):
            par = c & 1
            c0 = c * 128
            yl = b.op("sync", lambda s, c0=c0, par=par: s.dma_start(
                out=yt_sb[:, par * T:(par + 1) * T], in_=y_d[c0:c0 + 128, :]),
                inc=sY[par], inc_k=16, waits=[r_y[par]])
            hl = b.op("gpsimd", lambda g, c0=c0, par=par: g.dma_start(
                out=hsf_sb[:, par * T:(par + 1) * T], in_=hs_d[c0:c0 + 128, :]),
                inc=sHS[par], inc_k=16, waits=[r_hs[par]])
            gl = b.op("sync", lambda s, c0=c0, par=par: s.dma_start(
                out=gt_sb[:, par * T:(par + 1) * T], in_=gt_d[c0:c0 + 128, :]),
                inc=sGT[par], inc_k=16, waits=[r_gt[par]])
            dl = b.op("sync", lambda s, c0=c0, par=par: s.dma_start(
                out=dp_sb[:, par:par + 1], in_=col1(dp_d[c0:c0 + 128])),
                inc=sDP[par], inc_k=16, waits=[r_dp[par]])
            yt = yt_sb[:, par * T:(par + 1) * T]
            hsf = hsf_sb[:, par * T:(par + 1) * T]
            gt = gt_sb[:, par * T:(par + 1) * T]
            v_yf = b.op("vector", lambda v, yt=yt, hsf=hsf, par=par: v.scalar_tensor_tensor(
                yt[:, :], hsf[:, :], dp_sb[:, par:par + 1], yt[:, :], ALU.mult, ALU.add),
                inc=sDVE, waits=[yl, hl, dl, r_yt[par]])
            r_hs[par].set(v_yf)
            r_dp[par].set(v_yf)
            v_sg = b.op("scalar", lambda s, gt=gt: s.activation(out=sg_sb[:, :], in_=gt[:, :], func=AF.Silu),
                        inc=sACT, waits=[gl, r_sg])
            r_gt[par].set(v_sg)
            v_yo = b.op("vector", lambda v, c=c, yt=yt: v.tensor_mul(yo_sb[:, c * T:(c + 1) * T], yt[:, :], sg_sb[:, :]),
                        inc=sDVE, waits=[v_yf, v_sg])
            r_yt[par].set(v_yo)
            r_sg.set(v_yo)
            r_y[par].set(v_yf)
        st1.close()
        BAR1 = [sY[0].total(), sY[1].total(), sHS[0].total(), sHS[1].total(),
                sGT[0].total(), sGT[1].total(), sDP[0].total(), sDP[1].total(),
                (sDVE, sDVE.n), (sACT, sACT.n)]
        xt_sb = st2.enter_context(nc.sbuf_tensor("xt_sb", [128, 2 * T], F32))
        xa_sb = st2.enter_context(nc.sbuf_tensor("xa_sb", [128, 2 * T], F32))
        sq_sb = st2.enter_context(nc.sbuf_tensor("sq_sb", [128, 2 * T], F32))
        wo_sb = st2.enter_context(nc.sbuf_tensor("wo_sb", [128, 2 * 128], BF16))

        for m in range(NM):
            par = m & 1
            m0 = m * 128
            xl = b.op("sync", lambda s, m0=m0, par=par: s.dma_start(
                out=xt_sb[:, par * T:(par + 1) * T], in_=x_d[m0:m0 + 128, :]),
                inc=sXT[par], inc_k=16, waits=[r_xt[par]] + BAR1)
            v_mm = None
            for c in range(NT):
                wpar = c & 1
                wl = b.op("gpsimd", lambda g, c=c, m0=m0, wpar=wpar: g.dma_start(
                    out=wo_sb[:, wpar * 128:(wpar + 1) * 128], in_=wo_d[c * 128:(c + 1) * 128, m0:m0 + 128]),
                    inc=sWO[wpar], inc_k=16, waits=[r_wo[wpar]] + (BAR1 if m == 0 and c < 2 else []))
                for j in range(NCH):
                    v_mm = b.op("tensor", lambda t, c=c, j=j, par=par, wpar=wpar: t.matmul(
                        pM[par][:, j * NC:(j + 1) * NC], wo_sb[:, wpar * 128:(wpar + 1) * 128],
                        yo_sb[:, c * T + j * NC:c * T + (j + 1) * NC],
                        start=(c == 0), stop=(c == NT - 1)),
                        inc=sPE if j == NCH - 1 else None,
                        waits=([wl] + ([v_yo, r_pM[par]] if c == 0 else [])))
                r_wo[wpar].set(v_mm)
            xa = xa_sb[:, par * T:(par + 1) * T]
            sq = sq_sb[:, par * T:(par + 1) * T]
            v_xa = b.op("vector", lambda v, xa=xa, par=par: v.tensor_add(xa[:, :], pM[par][:, :], xt_sb[:, par * T:(par + 1) * T]),
                        inc=sDVE, waits=[v_mm, xl, r_xa[par]])
            r_pM[par].set(v_xa)
            r_xt[par].set(v_xa)
            v_xas = b.op("gpsimd", lambda g, m0=m0, xa=xa: g.dma_start(out=xa_i[m0:m0 + 128, :], in_=xa[:, :]),
                         inc=sXA[par], inc_k=16, waits=[v_xa])
            v_sq = b.op("scalar", lambda s, xa=xa, sq=sq: s.activation(out=sq[:, :], in_=xa[:, :], func=AF.Square),
                        inc=sACT, waits=[v_xa, r_sq[par]])
            r_xa[par].set(v_sq, v_xas)
            for j in range(NCH):
                v_s1 = b.op("tensor", lambda t, j=j, xa=xa, par=par: t.matmul(
                    pM[par][0:1, j * NC:(j + 1) * NC],
                    ones128_sb[:, 0:1], xa[:, j * NC:(j + 1) * NC], start=True, stop=True),
                    inc=sPE, waits=[v_xa, v_ones128, r_pM[par]])
                v_a1 = b.op("vector", lambda v, j=j, par=par: v.tensor_add(sum_sb[0:1, j * NC:(j + 1) * NC], sum_sb[0:1, j * NC:(j + 1) * NC], pM[par][0:1, j * NC:(j + 1) * NC]),
                            inc=sDVE, waits=[v_s1, v_sum0])
                v_s2 = b.op("tensor", lambda t, j=j, sq=sq, par=par: t.matmul(
                    pM[par][0:1, j * NC:(j + 1) * NC], ones128_sb[:, 0:1], sq[:, j * NC:(j + 1) * NC], start=True, stop=True),
                    inc=sPE, waits=[v_sq, v_a1])
                v_a2 = b.op("vector", lambda v, j=j, par=par: v.tensor_add(ssq_sb[0:1, j * NC:(j + 1) * NC], ssq_sb[0:1, j * NC:(j + 1) * NC], pM[par][0:1, j * NC:(j + 1) * NC]),
                            inc=sDVE, waits=[v_s2, v_ssq0])
            r_sq[par].set(v_s2)
            r_pM[par].set(v_a2)
        v_stats = v_a2
        st2.close()
        BAR2 = [sXT[0].total(), sXT[1].total(), sWO[0].total(), sWO[1].total(),
                sXA[0].total(), sXA[1].total(), (sPE, sPE.n), (sACT, sACT.n), (sDVE, sDVE.n)]
        xab_sb = st.enter_context(nc.sbuf_tensor("xab_sb", [128, 2 * T], F32))
        x1_sb = st.enter_context(nc.sbuf_tensor("x1_sb", [128, 2 * T], BF16))
        mubc_sb = st.enter_context(nc.sbuf_tensor("mubc_sb", [128, T], F32))
        rsbc_sb = st.enter_context(nc.sbuf_tensor("rsbc_sb", [128, T], F32))

        v_mu = b.op("vector", lambda v: v.tensor_scalar(sum_sb[0:1, :], sum_sb[0:1, :], 1.0 / DM, None, ALU.mult),
                    inc=sDVE, waits=[v_stats])
        v_m2 = b.op("vector", lambda v: v.tensor_mul(tmp_sb[0:1, :], sum_sb[0:1, :], sum_sb[0:1, :]),
                    inc=sDVE, waits=[v_mu])
        v_var = b.op("vector", lambda v: v.scalar_tensor_tensor(ssq_sb[0:1, :], ssq_sb[0:1, :], 1.0 / DM, tmp_sb[0:1, :], ALU.mult, ALU.subtract),
                     inc=sDVE, waits=[v_m2])
        v_ln = b.op("scalar", lambda s: s.activation(out=ssq_sb[0:1, :], in_=ssq_sb[0:1, :], func=AF.Ln, bias=1e-5),
                    inc=sACT, waits=[v_var])
        v_rs = b.op("scalar", lambda s: s.activation(out=ssq_sb[0:1, :], in_=ssq_sb[0:1, :], func=AF.Exp, scale=-0.5),
                    inc=sACT, waits=[v_ln])
        v_mbc = None
        for j in range(NCH):
            v_bm = b.op("tensor", lambda t, j=j: t.matmul(
                pM[0][:, j * NC:(j + 1) * NC], ones1_sb[0:1, :], sum_sb[0:1, j * NC:(j + 1) * NC], start=True, stop=True),
                inc=sPE, waits=[v_mu, v_ones1, r_pM[0]])
            v_mbc = b.op("vector", lambda v, j=j: v.tensor_copy(mubc_sb[:, j * NC:(j + 1) * NC], pM[0][:, j * NC:(j + 1) * NC]),
                         inc=sDVE, waits=[v_bm] + BAR2)
            r_pM[0].set(v_mbc)
        v_rbc = None
        for j in range(NCH):
            v_rm = b.op("tensor", lambda t, j=j: t.matmul(
                pM[1][:, j * NC:(j + 1) * NC], ones1_sb[0:1, :], ssq_sb[0:1, j * NC:(j + 1) * NC], start=True, stop=True),
                inc=sPE, waits=[v_rs, v_ones1, r_pM[1]])
            v_rbc = b.op("vector", lambda v, j=j: v.tensor_copy(rsbc_sb[:, j * NC:(j + 1) * NC], pM[1][:, j * NC:(j + 1) * NC]),
                         inc=sDVE, waits=[v_rm] + BAR2)
            r_pM[1].set(v_rbc)
        for m in range(NM):
            par = m & 1
            m0 = m * 128
            al = b.op("gpsimd", lambda g, m0=m0, par=par: g.dma_start(
                out=xab_sb[:, par * T:(par + 1) * T], in_=xa_i[m0:m0 + 128, :]),
                inc=sXAL[par], inc_k=16, waits=[r_xab[par], sXA[0].total(), sXA[1].total()] + (BAR2 if m < 2 else []))
            xab = xab_sb[:, par * T:(par + 1) * T]
            x1t = x1_sb[:, par * T:(par + 1) * T]
            v_n1 = b.op("vector", lambda v, xab=xab: v.tensor_sub(xab[:, :], xab[:, :], mubc_sb[:, :]),
                        inc=sDVE, waits=[al, v_mbc])
            v_n2 = b.op("vector", lambda v, xab=xab, x1t=x1t: v.tensor_mul(x1t[:, :], xab[:, :], rsbc_sb[:, :]),
                        inc=sDVE, waits=[v_n1, v_rbc, r_x1[par]])
            r_xab[par].set(v_n2)
            v_st = b.op("gpsimd", lambda g, m0=m0, x1t=x1t: g.dma_start(out=x1_d[m0:m0 + 128, :], in_=x1t[:, :]),
                        inc=sX1[par], inc_k=16, waits=[v_n2])
            r_x1[par].set(v_st)

        b.op("gpsimd", lambda g: g.engine_nop(), waits=[sX1[0].total(), sX1[1].total()])
        b.finalize(block)
    return nc



def build_p4():
    nc = bass.Bass("TRN2", target_bir_lowering=False)
    RG = DM // NCORES
    RD = GH // NCORES

    gi_d = nc.dram_tensor("gidxT", [RG, GH], I32, kind="ExternalInput").ap()
    ui_d = nc.dram_tensor("uidxT", [RG, GH], I32, kind="ExternalInput").ap()
    di_d = nc.dram_tensor("didxT", [RD, DM], I32, kind="ExternalInput").ap()
    lut_d = nc.dram_tensor("lut", [16], F32, kind="ExternalInput").ap()
    wg_d = nc.dram_tensor("wgt", [RG, GH], BF16, kind="ExternalOutput").ap()
    wu_d = nc.dram_tensor("wut", [RG, GH], BF16, kind="ExternalOutput").ap()
    wd_d = nc.dram_tensor("wdt", [RD, DM], BF16, kind="ExternalOutput").ap()

    st = ExitStack()
    with st:
        lut_sb = st.enter_context(nc.sbuf_tensor("lut_sb", [128, 16], F32))
        idx_sb = st.enter_context(nc.sbuf_tensor("idx_sb", [128, 2 * GH], F32))
        acc_sb = st.enter_context(nc.sbuf_tensor("acc_sb", [128, 2 * GH], F32))
        tmp_sb = st.enter_context(nc.sbuf_tensor("tmp_sb", [128, GH], F32))
        out_sb = st.enter_context(nc.sbuf_tensor("out_sb", [128, 2 * GH], BF16))

        b = B(nc, st)
        sDVE, sPOOL = b.sem("sDVE"), b.sem("sPOOL")
        sIDX = b.sems(2, "sIDX")
        sOUT = b.sems(2, "sOUT")
        sLUT = b.sem("sLUT")
        block = st.enter_context(nc.Block())

        r_idx = [Res(), Res()]
        r_acc = [Res(), Res()]
        r_tmp = Res()
        r_out = [Res(), Res()]

        ll = b.op("gpsimd", lambda g: g.dma_start(
            out=lut_sb[:, :], in_=lut_d.unsqueeze(0).partition_broadcast(128)),
            inc=sLUT, inc_k=16)

        jobs = []
        for (idx_d, w_d, R, C) in [(gi_d, wg_d, RG, GH), (ui_d, wu_d, RG, GH), (di_d, wd_d, RD, DM)]:
            r0 = 0
            while r0 < R:
                p = min(128, R - r0)
                jobs.append((idx_d, w_d, r0, p, C))
                r0 += 128
        ji = 0
        for (idx_d, w_d, r0, p, C) in jobs:
            par = ji & 1
            ji += 1
            il = b.op("gpsimd", lambda g, idx_d=idx_d, r0=r0, p=p, C=C, par=par: g.dma_start(
                out=idx_sb[0:p, par * GH:par * GH + C], in_=idx_d[r0:r0 + p, :]),
                inc=sIDX[par], inc_k=16, waits=[r_idx[par]])
            acc = acc_sb[:, par * GH:par * GH + C]
            vk = b.op("vector", lambda v, p=p, C=C, par=par, acc=acc: v.tensor_scalar(
                acc[0:p, :], idx_sb[0:p, par * GH:par * GH + C], 0.0, lut_sb[0:p, 0:1], ALU.is_equal, ALU.mult),
                inc=sDVE, waits=[il, ll, r_acc[par]])
            for k in range(1, 16):
                vt = b.op("vector", lambda v, p=p, C=C, k=k, par=par: v.tensor_scalar(
                    tmp_sb[0:p, 0:C], idx_sb[0:p, par * GH:par * GH + C], float(k), lut_sb[0:p, k:k + 1], ALU.is_equal, ALU.mult),
                    inc=sDVE, waits=[vk, r_tmp])
                vk = b.op("vector", lambda v, p=p, C=C, par=par, acc=acc: v.tensor_add(
                    acc[0:p, :], acc[0:p, :], tmp_sb[0:p, 0:C]),
                    inc=sDVE, waits=[vt])
                r_tmp.set(vk)
            r_idx[par].set(vk)
            vo = b.op("vector", lambda v, p=p, C=C, par=par, acc=acc: v.tensor_copy(
                out_sb[0:p, par * GH:par * GH + C], acc[0:p, :]),
                inc=sDVE, waits=[vk, r_out[par]])
            r_acc[par].set(vo)
            vs = b.op("gpsimd", lambda g, w_d=w_d, r0=r0, p=p, C=C, par=par: g.dma_start(
                out=w_d[r0:r0 + p, :], in_=out_sb[0:p, par * GH:par * GH + C]),
                inc=sOUT[par], inc_k=16, waits=[vo])
            r_out[par].set(vs)

        b.op("gpsimd", lambda g: g.engine_nop(), waits=[sOUT[0].total(), sOUT[1].total()])
        b.finalize(block)
    return nc



def build_p5():
    nc = bass.Bass("TRN2", target_bir_lowering=False)
    T = TPC
    TH = 1024
    NHF = T // TH
    NC = 512
    NO = GH // 128
    NM = DM // 128

    x1_d = nc.dram_tensor("x1", [DM, T], BF16, kind="ExternalInput").ap()
    wg_d = nc.dram_tensor("wg_f", [DM, GH], BF16, kind="ExternalInput").ap()
    wu_d = nc.dram_tensor("wu_f", [DM, GH], BF16, kind="ExternalInput").ap()
    wd_d = nc.dram_tensor("wd_f", [GH, DM], BF16, kind="ExternalInput").ap()
    x2_d = nc.dram_tensor("x2", [DM, T], BF16, kind="ExternalOutput").ap()
    xa_i = nc.dram_tensor("xa2_i", [DM, T], F32, kind="Internal").ap()

    st = ExitStack()
    with st:
        csts_sb = st.enter_context(nc.sbuf_tensor("csts_sb", [128, 8], F32))
        ones1_sb = st.enter_context(nc.sbuf_tensor("ones1_sb", [1, 128], F32))
        ones128_sb = st.enter_context(nc.sbuf_tensor("ones128_sb", [128, 1], F32))
        sum_sb = st.enter_context(nc.sbuf_tensor("sum_sb", [1, TH], F32))
        ssq_sb = st.enter_context(nc.sbuf_tensor("ssq_sb", [1, TH], F32))
        tmp_sb = st.enter_context(nc.sbuf_tensor("tmp_sb", [1, TH], F32))
        x1_sb = st.enter_context(nc.sbuf_tensor("x1_sb", [128, 16 * TH], BF16))
        h_sb = st.enter_context(nc.sbuf_tensor("h_sb", [128, NO * TH], BF16))
        wg_sb = st.enter_context(nc.sbuf_tensor("wg_sb", [128, 2 * 2048], BF16))
        wu_sb = st.enter_context(nc.sbuf_tensor("wu_sb", [128, 2 * 2048], BF16))
        wd_sb = st.enter_context(nc.sbuf_tensor("wd_sb", [128, 2 * 128], BF16))
        sg_sb = st.enter_context(nc.sbuf_tensor("sg_sb", [128, TH], F32))
        xa_sb = st.enter_context(nc.sbuf_tensor("xa_sb", [128, 2 * TH], F32))
        sq_sb = st.enter_context(nc.sbuf_tensor("sq_sb", [128, 2 * TH], F32))
        mubc_sb = st.enter_context(nc.sbuf_tensor("mubc_sb", [128, TH], F32))
        rsbc_sb = st.enter_context(nc.sbuf_tensor("rsbc_sb", [128, TH], F32))
        xab_sb = st.enter_context(nc.sbuf_tensor("xab_sb", [128, 2 * TH], F32))
        x2_sb = st.enter_context(nc.sbuf_tensor("x2_sb", [128, 2 * TH], BF16))

        pG = st.enter_context(nc.psum_tensor("pG", [128, TH], F32))
        pU = st.enter_context(nc.psum_tensor("pU", [128, TH], F32))
        pD = [st.enter_context(nc.psum_tensor(f"pD{i}", [128, TH], F32)) for i in range(2)]

        b = B(nc, st)
        sPE, sACT, sDVE, sPOOL = b.sem("sPE"), b.sem("sACT"), b.sem("sDVE"), b.sem("sPOOL")
        sX1 = b.sems(2, "sX1")
        sWGL = b.sems(2, "sWGL"); sWUL = b.sems(2, "sWUL"); sWDL = b.sems(2, "sWDL")
        sXA = b.sems(2, "sXA"); sXAL = b.sems(2, "sXAL"); sX2 = b.sems(2, "sX2")
        block = st.enter_context(nc.Block())

        _cdeps = setup_consts(nc, csts_sb, b, sPOOL, [1e-5, 1.0 / DM, -0.5])
        v_ones1 = b.op("gpsimd", lambda g: g.memset(ones1_sb[:, :], 1.0), inc=sPOOL)
        v_ones128 = b.op("gpsimd", lambda g: g.memset(ones128_sb[:, :], 1.0), inc=sPOOL)

        r_x1 = [Res(), Res()]
        r_wg = [Res(), Res()]; r_wu = [Res(), Res()]; r_wd = [Res(), Res()]
        r_pG = Res(); r_pU = Res(); r_pD = [Res(), Res()]
        r_sg = Res(); r_h = Res()
        r_xa = [Res(), Res()]; r_sq = [Res(), Res()]
        r_xab = [Res(), Res()]; r_x2 = [Res(), Res()]
        r_sum = Res(); r_ssq = Res()

        for hf in range(NHF):
            th0 = hf * TH
            for k in range(16):
                xl = b.op("sync", lambda s, k=k, th0=th0, kpar=k & 1: s.dma_start(
                    out=x1_sb[:, k * TH:(k + 1) * TH], in_=x1_d[k * 128:(k + 1) * 128, th0:th0 + TH]),
                    inc=sX1[0], inc_k=16, waits=[r_x1[0]] if k == 0 else [])
            v_x1 = sX1[0].total()
            v_zs = b.op("gpsimd", lambda g: g.memset(sum_sb[:, :], 0.0), inc=sPOOL, waits=[r_sum])
            v_zq = b.op("gpsimd", lambda g: g.memset(ssq_sb[:, :], 0.0), inc=sPOOL, waits=[r_ssq])

            v_h = None
            for o in range(NO):
                par = o & 1
                o0 = o * 128
                for k in range(16):
                    gl = b.op("gpsimd", lambda g, k=k, o0=o0, par=par: g.dma_start(
                        out=wg_sb[:, par * 2048 + k * 128:par * 2048 + (k + 1) * 128],
                        in_=wg_d[k * 128:(k + 1) * 128, o0:o0 + 128]),
                        inc=sWGL[par], inc_k=16, waits=[r_wg[par]] if k == 0 else [])
                gl = sWGL[par].total()
                for k in range(16):
                    ul = b.op("gpsimd", lambda g, k=k, o0=o0, par=par: g.dma_start(
                        out=wu_sb[:, par * 2048 + k * 128:par * 2048 + (k + 1) * 128],
                        in_=wu_d[k * 128:(k + 1) * 128, o0:o0 + 128]),
                        inc=sWUL[par], inc_k=16, waits=[r_wu[par]] if k == 0 else [])
                ul = sWUL[par].total()
                v_gm = None
                for j in range(TH // NC):
                    for k in range(16):
                        v_gm = b.op("tensor", lambda t, k=k, j=j, par=par: t.matmul(
                            pG[:, j * NC:(j + 1) * NC], wg_sb[:, par * 2048 + k * 128:par * 2048 + (k + 1) * 128],
                            x1_sb[:, k * TH + j * NC:k * TH + (j + 1) * NC],
                            start=(k == 0), stop=(k == 15)),
                            inc=sPE if k == 15 else None,
                            waits=([gl, (v_x1[0], v_x1[1]), r_pG] if j == 0 and k == 0 else []))
                v_um = None
                for j in range(TH // NC):
                    for k in range(16):
                        v_um = b.op("tensor", lambda t, k=k, j=j, par=par: t.matmul(
                            pU[:, j * NC:(j + 1) * NC], wu_sb[:, par * 2048 + k * 128:par * 2048 + (k + 1) * 128],
                            x1_sb[:, k * TH + j * NC:k * TH + (j + 1) * NC],
                            start=(k == 0), stop=(k == 15)),
                            inc=sPE if k == 15 else None,
                            waits=([ul, r_pU] if j == 0 and k == 0 else []))
                r_wg[par].set(v_gm)
                r_wu[par].set(v_um)
                v_sg = b.op("scalar", lambda s: s.activation(out=sg_sb[:, :], in_=pG[:, :], func=AF.Silu),
                            inc=sACT, waits=[v_gm, r_sg])
                r_pG.set(v_sg)
                v_h = b.op("vector", lambda v, o=o: v.tensor_mul(h_sb[:, o * TH:(o + 1) * TH], sg_sb[:, :], pU[:, :]),
                           inc=sDVE, waits=[v_sg, v_um, r_h])
                r_pU.set(v_h)
                r_sg.set(v_h)
            v_h_all = v_h
            v_a2 = None
            for m in range(NM):
                par = m & 1
                m0 = m * 128
                v_dm = None
                for o in range(NO):
                    wpar = o & 1
                    wl = b.op("gpsimd", lambda g, o=o, m0=m0, wpar=wpar: g.dma_start(
                        out=wd_sb[:, wpar * 128:(wpar + 1) * 128], in_=wd_d[o * 128:(o + 1) * 128, m0:m0 + 128]),
                        inc=sWDL[wpar], inc_k=16, waits=[r_wd[wpar]])
                    for j in range(TH // NC):
                        v_dm = b.op("tensor", lambda t, o=o, j=j, par=par, wpar=wpar: t.matmul(
                            pD[par][:, j * NC:(j + 1) * NC], wd_sb[:, wpar * 128:(wpar + 1) * 128],
                            h_sb[:, o * TH + j * NC:o * TH + (j + 1) * NC],
                            start=(o == 0), stop=(o == NO - 1)),
                            inc=sPE if j == (TH // NC) - 1 else None,
                            waits=([wl] + ([v_h_all, r_pD[par]] if o == 0 else [])))
                    r_wd[wpar].set(v_dm)
                xa = xa_sb[:, par * TH:(par + 1) * TH]
                sq = sq_sb[:, par * TH:(par + 1) * TH]
                v_xa = b.op("vector", lambda v, m=m, xa=xa, par=par: v.tensor_add(
                    xa[:, :], pD[par][:, :], x1_sb[:, m * TH:(m + 1) * TH]),
                    inc=sDVE, waits=[v_dm, (v_x1[0], v_x1[1]), r_xa[par]])
                r_pD[par].set(v_xa)
                v_xas = b.op("gpsimd", lambda g, m0=m0, th0=th0, xa=xa: g.dma_start(
                    out=xa_i[m0:m0 + 128, th0:th0 + TH], in_=xa[:, :]),
                    inc=sXA[par], inc_k=16, waits=[v_xa])
                v_sq = b.op("scalar", lambda s, xa=xa, sq=sq: s.activation(out=sq[:, :], in_=xa[:, :], func=AF.Square),
                            inc=sACT, waits=[v_xa, r_sq[par]])
                r_xa[par].set(v_sq, v_xas)
                for j in range(TH // NC):
                    v_s1 = b.op("tensor", lambda t, j=j, xa=xa, par=par: t.matmul(
                        pD[par][0:1, j * NC:(j + 1) * NC], ones128_sb[:, 0:1], xa[:, j * NC:(j + 1) * NC], start=True, stop=True),
                        inc=sPE, waits=[v_xa, v_ones128, r_pD[par]])
                    v_a1 = b.op("vector", lambda v, j=j, par=par: v.tensor_add(
                        sum_sb[0:1, j * NC:(j + 1) * NC], sum_sb[0:1, j * NC:(j + 1) * NC], pD[par][0:1, j * NC:(j + 1) * NC]),
                        inc=sDVE, waits=[v_s1, v_zs])
                    v_s2 = b.op("tensor", lambda t, j=j, sq=sq, par=par: t.matmul(
                        pD[par][0:1, j * NC:(j + 1) * NC], ones128_sb[:, 0:1], sq[:, j * NC:(j + 1) * NC], start=True, stop=True),
                        inc=sPE, waits=[v_sq, v_a1])
                    v_a2 = b.op("vector", lambda v, j=j, par=par: v.tensor_add(
                        ssq_sb[0:1, j * NC:(j + 1) * NC], ssq_sb[0:1, j * NC:(j + 1) * NC], pD[par][0:1, j * NC:(j + 1) * NC]),
                        inc=sDVE, waits=[v_s2, v_zq])
                r_sq[par].set(v_s2)
                r_pD[par].set(v_a2)
            v_stats = v_a2
            r_x1[0].set(v_xa, v_gm, v_um)

            v_mu = b.op("vector", lambda v: v.tensor_scalar(sum_sb[0:1, :], sum_sb[0:1, :], 1.0 / DM, None, ALU.mult),
                        inc=sDVE, waits=[v_stats])
            v_m2 = b.op("vector", lambda v: v.tensor_mul(tmp_sb[0:1, :], sum_sb[0:1, :], sum_sb[0:1, :]),
                        inc=sDVE, waits=[v_mu])
            v_var = b.op("vector", lambda v: v.scalar_tensor_tensor(ssq_sb[0:1, :], ssq_sb[0:1, :], 1.0 / DM, tmp_sb[0:1, :], ALU.mult, ALU.subtract),
                         inc=sDVE, waits=[v_m2])
            v_ln = b.op("scalar", lambda s: s.activation(out=ssq_sb[0:1, :], in_=ssq_sb[0:1, :], func=AF.Ln, bias=1e-5),
                        inc=sACT, waits=[v_var])
            v_rs = b.op("scalar", lambda s: s.activation(out=ssq_sb[0:1, :], in_=ssq_sb[0:1, :], func=AF.Exp, scale=-0.5),
                        inc=sACT, waits=[v_ln])
            v_mbc = None
            for j in range(TH // NC):
                v_bm = b.op("tensor", lambda t, j=j: t.matmul(
                    pD[0][:, j * NC:(j + 1) * NC], ones1_sb[0:1, :], sum_sb[0:1, j * NC:(j + 1) * NC], start=True, stop=True),
                    inc=sPE, waits=[v_mu, v_ones1, r_pD[0]])
                v_mbc = b.op("vector", lambda v, j=j: v.tensor_copy(mubc_sb[:, j * NC:(j + 1) * NC], pD[0][:, j * NC:(j + 1) * NC]),
                             inc=sDVE, waits=[v_bm])
                r_pD[0].set(v_mbc)
            v_rbc = None
            for j in range(TH // NC):
                v_rm = b.op("tensor", lambda t, j=j: t.matmul(
                    pD[1][:, j * NC:(j + 1) * NC], ones1_sb[0:1, :], ssq_sb[0:1, j * NC:(j + 1) * NC], start=True, stop=True),
                    inc=sPE, waits=[v_rs, v_ones1, r_pD[1]])
                v_rbc = b.op("vector", lambda v, j=j: v.tensor_copy(rsbc_sb[:, j * NC:(j + 1) * NC], pD[1][:, j * NC:(j + 1) * NC]),
                             inc=sDVE, waits=[v_rm])
                r_pD[1].set(v_rbc)
            r_sum.set(v_mbc)
            r_ssq.set(v_rbc)
            for m in range(NM):
                par = m & 1
                m0 = m * 128
                al = b.op("gpsimd", lambda g, m0=m0, th0=th0, par=par: g.dma_start(
                    out=xab_sb[:, par * TH:(par + 1) * TH], in_=xa_i[m0:m0 + 128, th0:th0 + TH]),
                    inc=sXAL[par], inc_k=16, waits=[r_xab[par], sXA[0].total(), sXA[1].total()])
                xab = xab_sb[:, par * TH:(par + 1) * TH]
                x2t = x2_sb[:, par * TH:(par + 1) * TH]
                v_n1 = b.op("vector", lambda v, xab=xab: v.tensor_sub(xab[:, :], xab[:, :], mubc_sb[:, :]),
                            inc=sDVE, waits=[al, v_mbc])
                v_n2 = b.op("vector", lambda v, xab=xab, x2t=x2t: v.tensor_mul(x2t[:, :], xab[:, :], rsbc_sb[:, :]),
                            inc=sDVE, waits=[v_n1, v_rbc, r_x2[par]])
                r_xab[par].set(v_n2)
                v_st = b.op("gpsimd", lambda g, m0=m0, th0=th0, x2t=x2t: g.dma_start(
                    out=x2_d[m0:m0 + 128, th0:th0 + TH], in_=x2t[:, :]),
                    inc=sX2[par], inc_k=16, waits=[v_n2])
                r_x2[par].set(v_st)
            r_h.set(v_dm)

        b.op("gpsimd", lambda g: g.engine_nop(), waits=[sX2[0].total(), sX2[1].total()])
        b.finalize(block)
    return nc



def build_p6a():
    nc = bass.Bass("TRN2", target_bir_lowering=False)
    T = TPC
    NC = 512
    NCH = T // NC
    NM = DM // 128
    R = 8

    x2_d = nc.dram_tensor("x2", [DM, T], BF16, kind="ExternalInput").ap()
    g_d = nc.dram_tensor("gam", [DM], F32, kind="ExternalInput").ap()
    be_d = nc.dram_tensor("bet", [DM], F32, kind="ExternalInput").ap()
    wi_d = nc.dram_tensor("w_in2", [DM, R], F32, kind="ExternalInput").ap()
    bi_d = nc.dram_tensor("b_in2", [R], F32, kind="ExternalInput").ap()
    xn_d = nc.dram_tensor("xn", [DM, T], BF16, kind="ExternalOutput").ap()
    u_d = nc.dram_tensor("u", [R, T], F32, kind="ExternalOutput").ap()

    st = ExitStack()
    with st:
        csts_sb = st.enter_context(nc.sbuf_tensor("csts_sb", [128, 8], F32))
        ones1_sb = st.enter_context(nc.sbuf_tensor("ones1_sb", [1, 128], F32))
        ones128_sb = st.enter_context(nc.sbuf_tensor("ones128_sb", [128, 1], F32))
        sum_sb = st.enter_context(nc.sbuf_tensor("sum_sb", [1, T], F32))
        ssq_sb = st.enter_context(nc.sbuf_tensor("ssq_sb", [1, T], F32))
        tmp_sb = st.enter_context(nc.sbuf_tensor("tmp_sb", [1, T], F32))
        x2_sb = st.enter_context(nc.sbuf_tensor("x2_sb", [128, 2 * T], F32))
        sq_sb = st.enter_context(nc.sbuf_tensor("sq_sb", [128, 2 * T], F32))
        mubc_sb = st.enter_context(nc.sbuf_tensor("mubc_sb", [128, T], F32))
        rsbc_sb = st.enter_context(nc.sbuf_tensor("rsbc_sb", [128, T], F32))
        gb_sb = st.enter_context(nc.sbuf_tensor("gb_sb", [128, 2], F32))
        bb_sb = st.enter_context(nc.sbuf_tensor("bb_sb", [128, 2], F32))
        wi_sb = st.enter_context(nc.sbuf_tensor("wi_sb", [128, 2 * R], BF16))
        bi_sb = st.enter_context(nc.sbuf_tensor("bi_sb", [R, 1], F32))
        xnf_sb = st.enter_context(nc.sbuf_tensor("xnf_sb", [128, 2 * T], F32))
        xnb_sb = st.enter_context(nc.sbuf_tensor("xnb_sb", [128, 2 * T], BF16))
        u_sb = st.enter_context(nc.sbuf_tensor("u_sb", [R, T], F32))

        pS = st.enter_context(nc.psum_tensor("pS", [128, NC], F32))
        pU = st.enter_context(nc.psum_tensor("pU", [R, T], F32))

        b = B(nc, st)
        sPE, sACT, sDVE, sPOOL = b.sem("sPE"), b.sem("sACT"), b.sem("sDVE"), b.sem("sPOOL")
        sX2 = b.sems(2, "sX2"); sX2b = b.sems(2, "sX2b")
        sGB = b.sems(2, "sGB"); sBB = b.sems(2, "sBB"); sWI = b.sems(2, "sWI"); sBI = b.sem("sBI")
        sXN = b.sems(2, "sXN"); sU = b.sem("sU")
        block = st.enter_context(nc.Block())

        _cdeps = setup_consts(nc, csts_sb, b, sPOOL, [1e-5, 1.0 / DM, -0.5])
        v_ones1 = b.op("gpsimd", lambda g: g.memset(ones1_sb[:, :], 1.0), inc=sPOOL)
        v_ones128 = b.op("gpsimd", lambda g: g.memset(ones128_sb[:, :], 1.0), inc=sPOOL)
        v_zs = b.op("gpsimd", lambda g: g.memset(sum_sb[:, :], 0.0), inc=sPOOL)
        v_zq = b.op("gpsimd", lambda g: g.memset(ssq_sb[:, :], 0.0), inc=sPOOL)

        r_x2 = [Res(), Res()]; r_sq = [Res(), Res()]; r_pS = Res(); r_pU = Res()
        r_gb = [Res(), Res()]; r_bb = [Res(), Res()]; r_wi = [Res(), Res()]
        r_xnf = [Res(), Res()]; r_xnb = [Res(), Res()]

        v_a2 = None
        for m in range(NM):
            par = m & 1
            m0 = m * 128
            xl = b.op("gpsimd", lambda g, m0=m0, par=par: g.dma_start(
                out=x2_sb[:, par * T:(par + 1) * T], in_=x2_d[m0:m0 + 128, :]),
                inc=sX2[par], inc_k=16, waits=[r_x2[par]])
            x2t = x2_sb[:, par * T:(par + 1) * T]
            sq = sq_sb[:, par * T:(par + 1) * T]
            v_sq = b.op("scalar", lambda s, x2t=x2t, sq=sq: s.activation(out=sq[:, :], in_=x2t[:, :], func=AF.Square),
                        inc=sACT, waits=[xl, r_sq[par]])
            for j in range(NCH):
                v_s1 = b.op("tensor", lambda t, j=j, x2t=x2t: t.matmul(
                    pS[0:1, 0:NC], ones128_sb[:, 0:1], x2t[:, j * NC:(j + 1) * NC], start=True, stop=True),
                    inc=sPE, waits=[xl, v_ones128, r_pS])
                v_a1 = b.op("vector", lambda v, j=j: v.tensor_add(sum_sb[0:1, j * NC:(j + 1) * NC], sum_sb[0:1, j * NC:(j + 1) * NC], pS[0:1, 0:NC]),
                            inc=sDVE, waits=[v_s1, v_zs])
                r_pS.set(v_a1)
                v_s2 = b.op("tensor", lambda t, j=j, sq=sq: t.matmul(
                    pS[0:1, 0:NC], ones128_sb[:, 0:1], sq[:, j * NC:(j + 1) * NC], start=True, stop=True),
                    inc=sPE, waits=[v_sq, v_a1, r_pS])
                v_a2 = b.op("vector", lambda v, j=j: v.tensor_add(ssq_sb[0:1, j * NC:(j + 1) * NC], ssq_sb[0:1, j * NC:(j + 1) * NC], pS[0:1, 0:NC]),
                            inc=sDVE, waits=[v_s2, v_zq])
                r_pS.set(v_a2)
            r_x2[par].set(v_s1, v_sq)
            r_sq[par].set(v_s2)
        v_stats = v_a2

        v_mu = b.op("vector", lambda v: v.tensor_scalar(sum_sb[0:1, :], sum_sb[0:1, :], 1.0 / DM, None, ALU.mult),
                    inc=sDVE, waits=[v_stats])
        v_m2 = b.op("vector", lambda v: v.tensor_mul(tmp_sb[0:1, :], sum_sb[0:1, :], sum_sb[0:1, :]),
                    inc=sDVE, waits=[v_mu])
        v_var = b.op("vector", lambda v: v.scalar_tensor_tensor(ssq_sb[0:1, :], ssq_sb[0:1, :], 1.0 / DM, tmp_sb[0:1, :], ALU.mult, ALU.subtract),
                     inc=sDVE, waits=[v_m2])
        v_ln = b.op("scalar", lambda s: s.activation(out=ssq_sb[0:1, :], in_=ssq_sb[0:1, :], func=AF.Ln, bias=1e-5),
                    inc=sACT, waits=[v_var])
        v_rs = b.op("scalar", lambda s: s.activation(out=ssq_sb[0:1, :], in_=ssq_sb[0:1, :], func=AF.Exp, scale=-0.5),
                    inc=sACT, waits=[v_ln])
        v_mbc = None
        for j in range(NCH):
            v_bm = b.op("tensor", lambda t, j=j: t.matmul(
                pS[:, 0:NC], ones1_sb[0:1, :], sum_sb[0:1, j * NC:(j + 1) * NC], start=True, stop=True),
                inc=sPE, waits=[v_mu, v_ones1, r_pS])
            v_mbc = b.op("vector", lambda v, j=j: v.tensor_copy(mubc_sb[:, j * NC:(j + 1) * NC], pS[:, 0:NC]),
                         inc=sDVE, waits=[v_bm])
            r_pS.set(v_mbc)
        v_rbc = None
        for j in range(NCH):
            v_rm = b.op("tensor", lambda t, j=j: t.matmul(
                pS[:, 0:NC], ones1_sb[0:1, :], ssq_sb[0:1, j * NC:(j + 1) * NC], start=True, stop=True),
                inc=sPE, waits=[v_rs, v_ones1, r_pS])
            v_rbc = b.op("vector", lambda v, j=j: v.tensor_copy(rsbc_sb[:, j * NC:(j + 1) * NC], pS[:, 0:NC]),
                         inc=sDVE, waits=[v_rm])
            r_pS.set(v_rbc)

        bil = b.op("sync", lambda s: s.dma_start(out=bi_sb[:, :], in_=col1(bi_d[:])), inc=sBI, inc_k=16)

        v_um = None
        for m in range(NM):
            par = m & 1
            m0 = m * 128
            xl = b.op("gpsimd", lambda g, m0=m0, par=par: g.dma_start(
                out=x2_sb[:, par * T:(par + 1) * T], in_=x2_d[m0:m0 + 128, :]),
                inc=sX2b[par], inc_k=16, waits=[r_x2[par]])
            gl = b.op("sync", lambda s, m0=m0, par=par: s.dma_start(
                out=gb_sb[:, par:par + 1], in_=col1(g_d[m0:m0 + 128])),
                inc=sGB[par], inc_k=16, waits=[r_gb[par]])
            bl = b.op("sync", lambda s, m0=m0, par=par: s.dma_start(
                out=bb_sb[:, par:par + 1], in_=col1(be_d[m0:m0 + 128])),
                inc=sBB[par], inc_k=16, waits=[r_bb[par]])
            wl = b.op("gpsimd", lambda g, m0=m0, par=par: g.dma_start(
                out=wi_sb[:, par * R:(par + 1) * R], in_=wi_d[m0:m0 + 128, :]),
                inc=sWI[par], inc_k=16, waits=[r_wi[par]])
            x2t = x2_sb[:, par * T:(par + 1) * T]
            xnf = xnf_sb[:, par * T:(par + 1) * T]
            xnb = xnb_sb[:, par * T:(par + 1) * T]
            v_n1 = b.op("vector", lambda v, x2t=x2t, xnf=xnf: v.tensor_sub(xnf[:, :], x2t[:, :], mubc_sb[:, :]),
                        inc=sDVE, waits=[xl, v_mbc, r_xnf[par]])
            r_x2[par].set(v_n1)
            v_n2 = b.op("vector", lambda v, xnf=xnf: v.tensor_mul(xnf[:, :], xnf[:, :], rsbc_sb[:, :]),
                        inc=sDVE, waits=[v_n1, v_rbc])
            v_n3 = b.op("vector", lambda v, xnf=xnf, xnb=xnb, par=par: v.tensor_scalar(
                xnb[:, :], xnf[:, :], gb_sb[:, par:par + 1], bb_sb[:, par:par + 1], ALU.mult, ALU.add),
                inc=sDVE, waits=[v_n2, gl, bl, r_xnb[par]])
            r_gb[par].set(v_n3)
            r_bb[par].set(v_n3)
            r_xnf[par].set(v_n3)
            v_xs = b.op("gpsimd", lambda g, m0=m0, xnb=xnb: g.dma_start(out=xn_d[m0:m0 + 128, :], in_=xnb[:, :]),
                        inc=sXN[par], inc_k=16, waits=[v_n3])
            for j in range(NCH):
                v_um = b.op("tensor", lambda t, m=m, j=j, par=par, xnb=xnb: t.matmul(
                    pU[:, j * NC:(j + 1) * NC], wi_sb[:, par * R:(par + 1) * R], xnb[:, j * NC:(j + 1) * NC],
                    start=(m == 0), stop=(m == NM - 1)),
                    inc=sPE if j == NCH - 1 else None,
                    waits=[v_n3, wl] + ([r_pU] if m == 0 and j == 0 else []))
            r_wi[par].set(v_um)
            r_xnb[par].set(v_um, v_xs)
        v_u = b.op("vector", lambda v: v.tensor_scalar(u_sb[:, :], pU[:, :], bi_sb[:, 0:1], None, ALU.add),
                   inc=sDVE, waits=[v_um, bil])
        r_pU.set(v_u)
        b.op("gpsimd", lambda g: g.dma_start(out=u_d[:, :], in_=u_sb[:, :]), inc=sU, inc_k=16, waits=[v_u])
        b.op("gpsimd", lambda g: g.engine_nop(), waits=[sU.total(), sXN[0].total(), sXN[1].total()])
        b.finalize(block)
    return nc



def build_p6b():
    nc = bass.Bass("TRN2", target_bir_lowering=False)
    T = TPC
    NC = 512
    NCH = T // NC
    NM = DM // 128
    R = 8
    RF = 32
    SEQ = 8192
    HOFF = SEQ - T

    u_d = nc.dram_tensor("u_sh", [R, SEQ], F32, kind="ExternalInput").ap()
    x2_d = nc.dram_tensor("x2", [DM, T], BF16, kind="ExternalInput").ap()
    xn_d = nc.dram_tensor("xn", [DM, T], BF16, kind="ExternalInput").ap()
    dl_d = nc.dram_tensor("dlog", [R], F32, kind="ExternalInput").ap()
    wo_d = nc.dram_tensor("w_out2", [R, DM], F32, kind="ExternalInput").ap()
    bo_d = nc.dram_tensor("b_out2", [DM], F32, kind="ExternalInput").ap()
    dsk_d = nc.dram_tensor("d_skip", [DM], F32, kind="ExternalInput").ap()
    ga_d = nc.dram_tensor("gam", [DM], F32, kind="ExternalInput").ap()
    be_d = nc.dram_tensor("bet", [DM], F32, kind="ExternalInput").ap()
    wd_d = nc.dram_tensor("w_down", [DM, RF], F32, kind="ExternalInput").ap()
    bd_d = nc.dram_tensor("b_down", [RF], F32, kind="ExternalInput").ap()
    wu_d = nc.dram_tensor("w_up", [RF, DM], F32, kind="ExternalInput").ap()
    bu_d = nc.dram_tensor("b_up", [DM], F32, kind="ExternalInput").ap()
    x3_d = nc.dram_tensor("x3", [DM, T], BF16, kind="ExternalOutput").ap()
    xt_i = nc.dram_tensor("xt_i", [DM, T], F32, kind="Internal").ap()
    tot_i = nc.dram_tensor("tot_i", [DM, T], F32, kind="Internal").ap()

    st = ExitStack()
    with st:
        csts_sb = st.enter_context(nc.sbuf_tensor("csts_sb", [128, 8], F32))
        ones1_sb = st.enter_context(nc.sbuf_tensor("ones1_sb", [1, 128], F32))
        ones128_sb = st.enter_context(nc.sbuf_tensor("ones128_sb", [128, 1], F32))
        sum_sb = st.enter_context(nc.sbuf_tensor("sum_sb", [1, T], F32))
        ssq_sb = st.enter_context(nc.sbuf_tensor("ssq_sb", [1, T], F32))
        tmp_sb = st.enter_context(nc.sbuf_tensor("tmp_sb", [1, T], F32))
        h_sb = st.enter_context(nc.sbuf_tensor("h_sb", [R, SEQ], F32))
        u_sb = st.enter_context(nc.sbuf_tensor("u_sb", [R, 2 * T], F32))
        dbc_sb = st.enter_context(nc.sbuf_tensor("dbc_sb", [R, T], F32))
        dec_sb = st.enter_context(nc.sbuf_tensor("dec_sb", [R, 1], F32))
        wo_sb = st.enter_context(nc.sbuf_tensor("wo_sb", [R, 2 * 128], F32))
        bo_sb = st.enter_context(nc.sbuf_tensor("bo_sb", [128, 2], F32))
        dsk_sb = st.enter_context(nc.sbuf_tensor("dsk_sb", [128, 2], F32))
        gb2_sb = st.enter_context(nc.sbuf_tensor("gb2_sb", [128, 2], F32))
        bb2_sb = st.enter_context(nc.sbuf_tensor("bb2_sb", [128, 2], F32))
        wdn_sb = st.enter_context(nc.sbuf_tensor("wdn_sb", [128, 2 * RF], F32))
        bdn_sb = st.enter_context(nc.sbuf_tensor("bdn_sb", [RF, 1], F32))
        wup_sb = st.enter_context(nc.sbuf_tensor("wup_sb", [RF, 2 * 128], F32))
        bup_sb = st.enter_context(nc.sbuf_tensor("bup_sb", [128, 2], F32))
        xn_sb = st.enter_context(nc.sbuf_tensor("xn_sb", [128, 2 * T], F32))
        x2_sb = st.enter_context(nc.sbuf_tensor("x2_sb", [128, 2 * T], F32))
        xt_sb = st.enter_context(nc.sbuf_tensor("xt_sb", [128, 2 * T], F32))
        sq_sb = st.enter_context(nc.sbuf_tensor("sq_sb", [128, 2 * T], F32))
        mubc_sb = st.enter_context(nc.sbuf_tensor("mubc_sb", [128, T], F32))
        rsbc_sb = st.enter_context(nc.sbuf_tensor("rsbc_sb", [128, T], F32))
        hf_sb = st.enter_context(nc.sbuf_tensor("hf_sb", [RF, T], F32))
        x3_sb = st.enter_context(nc.sbuf_tensor("x3_sb", [128, 2 * T], BF16))

        pA = st.enter_context(nc.psum_tensor("pA", [128, T], F32))
        pF = st.enter_context(nc.psum_tensor("pF", [RF, T], F32))

        b = B(nc, st)
        sPE, sACT, sDVE, sPOOL = b.sem("sPE"), b.sem("sACT"), b.sem("sDVE"), b.sem("sPOOL")
        sUL = b.sems(2, "sUL"); sMISC = b.sem("sMISC")
        sXN = b.sems(2, "sXN"); sX2 = b.sems(2, "sX2"); sXTL = b.sems(2, "sXTL")
        sWO = b.sems(2, "sWO"); sCOL = b.sems(2, "sCOL")
        sXTS = b.sems(2, "sXTS"); sTOT = b.sems(2, "sTOT"); sTOTL = b.sems(2, "sTOTL")
        sX3 = b.sems(2, "sX3")
        sWD = b.sems(2, "sWD"); sWU = b.sems(2, "sWU")
        block = st.enter_context(nc.Block())

        _cdeps = setup_consts(nc, csts_sb, b, sPOOL, [1e-5, 1.0 / DM, -0.5])
        v_ones1 = b.op("gpsimd", lambda g: g.memset(ones1_sb[:, :], 1.0), inc=sPOOL)
        v_ones128 = b.op("gpsimd", lambda g: g.memset(ones128_sb[:, :], 1.0), inc=sPOOL)
        v_zs = b.op("gpsimd", lambda g: g.memset(sum_sb[:, :], 0.0), inc=sPOOL)
        v_zq = b.op("gpsimd", lambda g: g.memset(ssq_sb[:, :], 0.0), inc=sPOOL)
        v_ones_d = b.op("gpsimd", lambda g: g.memset(dbc_sb[:, :], 1.0), inc=sPOOL)

        r_pA = Res(); r_pF = Res()
        r_u = [Res(), Res()]
        r_xn = [Res(), Res()]; r_x2 = [Res(), Res()]; r_xt = [Res(), Res()]; r_sq = [Res(), Res()]
        r_wo = [Res(), Res()]; r_cols = [Res(), Res()]
        r_wd = [Res(), Res()]; r_wu = [Res(), Res()]
        r_x3 = [Res(), Res()]

        dl = b.op("sync", lambda s: s.dma_start(out=dec_sb[:, :], in_=col1(dl_d[:])), inc=sMISC, inc_k=16)
        v_dec = b.op("scalar", lambda s: s.activation(out=dec_sb[:, :], in_=dec_sb[:, :], func=AF.Sigmoid),
                     inc=sACT, waits=[dl])
        v_dbc = b.op("vector", lambda v: v.tensor_scalar(dbc_sb[:, :], dbc_sb[:, :], dec_sb[:, 0:1], None, ALU.mult),
                     inc=sDVE, waits=[v_dec, v_ones_d])
        v_h = None
        for q in range(SEQ // T):
            qpar = q & 1
            ul = b.op("gpsimd", lambda g, q=q, qpar=qpar: g.dma_start(
                out=u_sb[:, qpar * T:(qpar + 1) * T], in_=u_d[:, q * T:(q + 1) * T]),
                inc=sUL[qpar], inc_k=16, waits=[r_u[qpar]])
            if q == 0:
                v_h = b.op("vector", lambda v, q=q, qpar=qpar: v.tensor_tensor_scan(
                    h_sb[:, q * T:(q + 1) * T], dbc_sb[:, :], u_sb[:, qpar * T:(qpar + 1) * T], 0.0, ALU.mult, ALU.add),
                    inc=sDVE, waits=[v_dbc, ul])
            else:
                v_h = b.op("vector", lambda v, q=q, qpar=qpar: v.tensor_tensor_scan(
                    h_sb[:, q * T:(q + 1) * T], dbc_sb[:, :], u_sb[:, qpar * T:(qpar + 1) * T],
                    h_sb[:, q * T - 1:q * T], ALU.mult, ALU.add),
                    inc=sDVE, waits=[v_h, ul])
            r_u[qpar].set(v_h)

        v_a2 = None
        for m in range(NM):
            par = m & 1
            m0 = m * 128
            wl = b.op("sync", lambda s, m0=m0, par=par: s.dma_start(
                out=wo_sb[:, par * 128:(par + 1) * 128], in_=wo_d[:, m0:m0 + 128]),
                inc=sWO[par], inc_k=16, waits=[r_wo[par]])
            cl1 = b.op("sync", lambda s, m0=m0, par=par: s.dma_start(
                out=bo_sb[:, par:par + 1], in_=col1(bo_d[m0:m0 + 128])),
                inc=sCOL[par], inc_k=16, waits=[r_cols[par]])
            cl2 = b.op("sync", lambda s, m0=m0, par=par: s.dma_start(
                out=dsk_sb[:, par:par + 1], in_=col1(dsk_d[m0:m0 + 128])),
                inc=sCOL[par], inc_k=16)
            xnl = b.op("gpsimd", lambda g, m0=m0, par=par: g.dma_start(
                out=xn_sb[:, par * T:(par + 1) * T], in_=xn_d[m0:m0 + 128, :]),
                inc=sXN[par], inc_k=16, waits=[r_xn[par]])
            x2l = b.op("gpsimd", lambda g, m0=m0, par=par: g.dma_start(
                out=x2_sb[:, par * T:(par + 1) * T], in_=x2_d[m0:m0 + 128, :]),
                inc=sX2[par], inc_k=16, waits=[r_x2[par]])
            v_mm = None
            for j in range(NCH):
                v_mm = b.op("tensor", lambda t, j=j, par=par: t.matmul(
                    pA[:, j * NC:(j + 1) * NC], wo_sb[:, par * 128:(par + 1) * 128],
                    h_sb[:, HOFF + j * NC:HOFF + (j + 1) * NC], start=True, stop=True),
                    inc=sPE if j == NCH - 1 else None,
                    waits=[wl, v_h] + ([r_pA] if j == 0 else []))
            r_wo[par].set(v_mm)
            xn = xn_sb[:, par * T:(par + 1) * T]
            x2t = x2_sb[:, par * T:(par + 1) * T]
            xt = xt_sb[:, par * T:(par + 1) * T]
            sq = sq_sb[:, par * T:(par + 1) * T]
            v_e1 = b.op("vector", lambda v, xn=xn, xt=xt, par=par: v.scalar_tensor_tensor(
                xt[:, :], xn[:, :], dsk_sb[:, par:par + 1], pA[:, :], ALU.mult, ALU.add),
                inc=sDVE, waits=[v_mm, xnl, cl1, cl2, r_xt[par]])
            r_pA.set(v_e1)
            r_xn[par].set(v_e1)
            v_e2 = b.op("vector", lambda v, xt=xt, par=par: v.tensor_scalar(
                xt[:, :], xt[:, :], bo_sb[:, par:par + 1], None, ALU.add),
                inc=sDVE, waits=[v_e1])
            r_cols[par].set(v_e2)
            v_e3 = b.op("vector", lambda v, xt=xt, x2t=x2t: v.tensor_add(xt[:, :], xt[:, :], x2t[:, :]),
                        inc=sDVE, waits=[v_e2, x2l])
            r_x2[par].set(v_e3)
            v_xts = b.op("gpsimd", lambda g, m0=m0, xt=xt: g.dma_start(out=xt_i[m0:m0 + 128, :], in_=xt[:, :]),
                         inc=sXTS[par], inc_k=16, waits=[v_e3])
            v_sq = b.op("scalar", lambda s, xt=xt, sq=sq: s.activation(out=sq[:, :], in_=xt[:, :], func=AF.Square),
                        inc=sACT, waits=[v_e3, r_sq[par]])
            r_xt[par].set(v_sq, v_xts)
            for j in range(NCH):
                v_s1 = b.op("tensor", lambda t, j=j, xt=xt: t.matmul(
                    pA[0:1, j * NC:(j + 1) * NC], ones128_sb[:, 0:1], xt[:, j * NC:(j + 1) * NC], start=True, stop=True),
                    inc=sPE, waits=[v_e3, v_ones128, r_pA])
                v_a1 = b.op("vector", lambda v, j=j: v.tensor_add(sum_sb[0:1, j * NC:(j + 1) * NC], sum_sb[0:1, j * NC:(j + 1) * NC], pA[0:1, j * NC:(j + 1) * NC]),
                            inc=sDVE, waits=[v_s1, v_zs])
                v_s2 = b.op("tensor", lambda t, j=j, sq=sq: t.matmul(
                    pA[0:1, j * NC:(j + 1) * NC], ones128_sb[:, 0:1], sq[:, j * NC:(j + 1) * NC], start=True, stop=True),
                    inc=sPE, waits=[v_sq, v_a1])
                v_a2 = b.op("vector", lambda v, j=j: v.tensor_add(ssq_sb[0:1, j * NC:(j + 1) * NC], ssq_sb[0:1, j * NC:(j + 1) * NC], pA[0:1, j * NC:(j + 1) * NC]),
                            inc=sDVE, waits=[v_s2, v_zq])
                r_pA.set(v_a2)
            r_sq[par].set(v_s2)
        v_stats = v_a2

        def ln_finalize(prefix, v_stats_dep, prev_bc):
            v_mu = b.op("vector", lambda v: v.tensor_scalar(sum_sb[0:1, :], sum_sb[0:1, :], 1.0 / DM, None, ALU.mult),
                        inc=sDVE, waits=[v_stats_dep])
            v_m2 = b.op("vector", lambda v: v.tensor_mul(tmp_sb[0:1, :], sum_sb[0:1, :], sum_sb[0:1, :]),
                        inc=sDVE, waits=[v_mu])
            v_var = b.op("vector", lambda v: v.scalar_tensor_tensor(ssq_sb[0:1, :], ssq_sb[0:1, :], 1.0 / DM, tmp_sb[0:1, :], ALU.mult, ALU.subtract),
                         inc=sDVE, waits=[v_m2])
            v_ln = b.op("scalar", lambda s: s.activation(out=ssq_sb[0:1, :], in_=ssq_sb[0:1, :], func=AF.Ln, bias=1e-5),
                        inc=sACT, waits=[v_var])
            v_rs = b.op("scalar", lambda s: s.activation(out=ssq_sb[0:1, :], in_=ssq_sb[0:1, :], func=AF.Exp, scale=-0.5),
                        inc=sACT, waits=[v_ln])
            v_mbc = None
            for j in range(NCH):
                v_bm = b.op("tensor", lambda t, j=j: t.matmul(
                    pA[:, j * NC:(j + 1) * NC], ones1_sb[0:1, :], sum_sb[0:1, j * NC:(j + 1) * NC], start=True, stop=True),
                    inc=sPE, waits=[v_mu, v_ones1, r_pA])
                v_mbc = b.op("vector", lambda v, j=j: v.tensor_copy(mubc_sb[:, j * NC:(j + 1) * NC], pA[:, j * NC:(j + 1) * NC]),
                             inc=sDVE, waits=[v_bm] + prev_bc)
                r_pA.set(v_mbc)
            v_rbc = None
            for j in range(NCH):
                v_rm = b.op("tensor", lambda t, j=j: t.matmul(
                    pA[:, j * NC:(j + 1) * NC], ones1_sb[0:1, :], ssq_sb[0:1, j * NC:(j + 1) * NC], start=True, stop=True),
                    inc=sPE, waits=[v_rs, v_ones1, r_pA])
                v_rbc = b.op("vector", lambda v, j=j: v.tensor_copy(rsbc_sb[:, j * NC:(j + 1) * NC], pA[:, j * NC:(j + 1) * NC]),
                             inc=sDVE, waits=[v_rm] + prev_bc)
                r_pA.set(v_rbc)
            return v_mbc, v_rbc

        v_mbc, v_rbc = ln_finalize("ln1", v_stats, [])

        bdl = b.op("sync", lambda s: s.dma_start(out=bdn_sb[:, :], in_=col1(bd_d[:])), inc=sMISC, inc_k=16)

        v_fm = None
        for m in range(NM):
            par = m & 1
            m0 = m * 128
            xtl = b.op("gpsimd", lambda g, m0=m0, par=par: g.dma_start(
                out=xt_sb[:, par * T:(par + 1) * T], in_=xt_i[m0:m0 + 128, :]),
                inc=sXTL[par], inc_k=16, waits=[r_xt[par], sXTS[0].total(), sXTS[1].total()])
            gl = b.op("sync", lambda s, m0=m0, par=par: s.dma_start(
                out=gb2_sb[:, par:par + 1], in_=col1(ga_d[m0:m0 + 128])),
                inc=sCOL[par], inc_k=16, waits=[r_cols[par]])
            bl = b.op("sync", lambda s, m0=m0, par=par: s.dma_start(
                out=bb2_sb[:, par:par + 1], in_=col1(be_d[m0:m0 + 128])),
                inc=sCOL[par], inc_k=16)
            wl = b.op("sync", lambda s, m0=m0, par=par: s.dma_start(
                out=wdn_sb[:, par * RF:(par + 1) * RF], in_=wd_d[m0:m0 + 128, :]),
                inc=sWD[par], inc_k=16, waits=[r_wd[par]])
            xt = xt_sb[:, par * T:(par + 1) * T]
            sq = sq_sb[:, par * T:(par + 1) * T]
            v_n1 = b.op("vector", lambda v, xt=xt, sq=sq: v.tensor_sub(sq[:, :], xt[:, :], mubc_sb[:, :]),
                        inc=sDVE, waits=[xtl, v_mbc, r_sq[par]])
            v_n2 = b.op("vector", lambda v, sq=sq: v.tensor_mul(sq[:, :], sq[:, :], rsbc_sb[:, :]),
                        inc=sDVE, waits=[v_n1, v_rbc])
            v_n3 = b.op("vector", lambda v, sq=sq, par=par: v.tensor_scalar(
                sq[:, :], sq[:, :], gb2_sb[:, par:par + 1], bb2_sb[:, par:par + 1], ALU.mult, ALU.add),
                inc=sDVE, waits=[v_n2, gl, bl])
            r_cols[par].set(v_n3)
            r_xt[par].set(v_n3)
            for j in range(NCH):
                v_fm = b.op("tensor", lambda t, m=m, j=j, par=par, sq=sq: t.matmul(
                    pF[:, j * NC:(j + 1) * NC], wdn_sb[:, par * RF:(par + 1) * RF], sq[:, j * NC:(j + 1) * NC],
                    start=(m == 0), stop=(m == NM - 1)),
                    inc=sPE if j == NCH - 1 else None,
                    waits=[v_n3, wl] + ([r_pF] if m == 0 and j == 0 else []))
            r_wd[par].set(v_fm)
            r_sq[par].set(v_fm)
        v_hf = b.op("vector", lambda v: v.tensor_scalar(hf_sb[:, :], pF[:, :], bdn_sb[:, 0:1], 0.0, ALU.add, ALU.max),
                    inc=sDVE, waits=[v_fm, bdl])
        r_pF.set(v_hf)

        v_zs2 = b.op("gpsimd", lambda g: g.memset(sum_sb[:, :], 0.0), inc=sPOOL, waits=[v_mbc])
        v_zq2 = b.op("gpsimd", lambda g: g.memset(ssq_sb[:, :], 0.0), inc=sPOOL, waits=[v_rbc])
        v_a2b = None
        for m in range(NM):
            par = m & 1
            m0 = m * 128
            wl = b.op("sync", lambda s, m0=m0, par=par: s.dma_start(
                out=wup_sb[:, par * 128:(par + 1) * 128], in_=wu_d[:, m0:m0 + 128]),
                inc=sWU[par], inc_k=16, waits=[r_wu[par]])
            bl = b.op("sync", lambda s, m0=m0, par=par: s.dma_start(
                out=bup_sb[:, par:par + 1], in_=col1(bu_d[m0:m0 + 128])),
                inc=sCOL[par], inc_k=16, waits=[r_cols[par]])
            xtl = b.op("gpsimd", lambda g, m0=m0, par=par: g.dma_start(
                out=xt_sb[:, par * T:(par + 1) * T], in_=xt_i[m0:m0 + 128, :]),
                inc=sXTL[par], inc_k=16, waits=[r_xt[par]])
            x2l = b.op("gpsimd", lambda g, m0=m0, par=par: g.dma_start(
                out=x2_sb[:, par * T:(par + 1) * T], in_=x2_d[m0:m0 + 128, :]),
                inc=sX2[par], inc_k=16, waits=[r_x2[par]])
            v_mm = None
            for j in range(NCH):
                v_mm = b.op("tensor", lambda t, j=j, par=par: t.matmul(
                    pA[:, j * NC:(j + 1) * NC], wup_sb[:, par * 128:(par + 1) * 128],
                    hf_sb[:, j * NC:(j + 1) * NC], start=True, stop=True),
                    inc=sPE if j == NCH - 1 else None,
                    waits=[wl, v_hf] + ([r_pA] if j == 0 else []))
            r_wu[par].set(v_mm)
            xt = xt_sb[:, par * T:(par + 1) * T]
            x2t = x2_sb[:, par * T:(par + 1) * T]
            sq = sq_sb[:, par * T:(par + 1) * T]
            v_e1 = b.op("vector", lambda v, xt=xt: v.scalar_tensor_tensor(
                xt[:, :], xt[:, :], 1.0, pA[:, :], ALU.mult, ALU.add),
                inc=sDVE, waits=[v_mm, xtl, r_xt[par]])
            r_pA.set(v_e1)
            v_e2 = b.op("vector", lambda v, xt=xt, par=par: v.tensor_scalar(
                xt[:, :], xt[:, :], bup_sb[:, par:par + 1], None, ALU.add),
                inc=sDVE, waits=[v_e1, bl])
            r_cols[par].set(v_e2)
            v_e3 = b.op("vector", lambda v, xt=xt, x2t=x2t: v.tensor_add(xt[:, :], xt[:, :], x2t[:, :]),
                        inc=sDVE, waits=[v_e2, x2l])
            r_x2[par].set(v_e3)
            v_ts = b.op("gpsimd", lambda g, m0=m0, xt=xt: g.dma_start(out=tot_i[m0:m0 + 128, :], in_=xt[:, :]),
                        inc=sTOT[par], inc_k=16, waits=[v_e3])
            v_sq = b.op("scalar", lambda s, xt=xt, sq=sq: s.activation(out=sq[:, :], in_=xt[:, :], func=AF.Square),
                        inc=sACT, waits=[v_e3, r_sq[par]])
            r_xt[par].set(v_sq, v_ts)
            for j in range(NCH):
                v_s1 = b.op("tensor", lambda t, j=j, xt=xt: t.matmul(
                    pA[0:1, j * NC:(j + 1) * NC], ones128_sb[:, 0:1], xt[:, j * NC:(j + 1) * NC], start=True, stop=True),
                    inc=sPE, waits=[v_e3, v_ones128, r_pA])
                v_a1 = b.op("vector", lambda v, j=j: v.tensor_add(sum_sb[0:1, j * NC:(j + 1) * NC], sum_sb[0:1, j * NC:(j + 1) * NC], pA[0:1, j * NC:(j + 1) * NC]),
                            inc=sDVE, waits=[v_s1, v_zs2])
                v_s2 = b.op("tensor", lambda t, j=j, sq=sq: t.matmul(
                    pA[0:1, j * NC:(j + 1) * NC], ones128_sb[:, 0:1], sq[:, j * NC:(j + 1) * NC], start=True, stop=True),
                    inc=sPE, waits=[v_sq, v_a1])
                v_a2b = b.op("vector", lambda v, j=j: v.tensor_add(ssq_sb[0:1, j * NC:(j + 1) * NC], ssq_sb[0:1, j * NC:(j + 1) * NC], pA[0:1, j * NC:(j + 1) * NC]),
                            inc=sDVE, waits=[v_s2, v_zq2])
                r_pA.set(v_a2b)
            r_sq[par].set(v_s2)
        v_stats2 = v_a2b

        v_mbc2, v_rbc2 = ln_finalize("ln2", v_stats2, [v_mbc, v_rbc])
        for m in range(NM):
            par = m & 1
            m0 = m * 128
            tl = b.op("gpsimd", lambda g, m0=m0, par=par: g.dma_start(
                out=xt_sb[:, par * T:(par + 1) * T], in_=tot_i[m0:m0 + 128, :]),
                inc=sTOTL[par], inc_k=16, waits=[r_xt[par], sTOT[0].total(), sTOT[1].total()])
            xt = xt_sb[:, par * T:(par + 1) * T]
            x3t = x3_sb[:, par * T:(par + 1) * T]
            v_n1 = b.op("vector", lambda v, xt=xt: v.tensor_sub(xt[:, :], xt[:, :], mubc_sb[:, :]),
                        inc=sDVE, waits=[tl, v_mbc2])
            v_n2 = b.op("vector", lambda v, xt=xt, x3t=x3t: v.tensor_mul(x3t[:, :], xt[:, :], rsbc_sb[:, :]),
                        inc=sDVE, waits=[v_n1, v_rbc2, r_x3[par]])
            r_xt[par].set(v_n2)
            v_st = b.op("gpsimd", lambda g, m0=m0, x3t=x3t: g.dma_start(out=x3_d[m0:m0 + 128, :], in_=x3t[:, :]),
                        inc=sX3[par], inc_k=16, waits=[v_n2])
            r_x3[par].set(v_st)

        b.op("gpsimd", lambda g: g.engine_nop(), waits=[sX3[0].total(), sX3[1].total()])
        b.finalize(block)
    return nc



def build_p7():
    nc = bass.Bass("TRN2", target_bir_lowering=False)
    T = TPC
    NC = 512
    NCH = T // NC
    E = MOE_E
    NTT = T // 128

    x3_d = nc.dram_tensor("x3", [DM, T], BF16, kind="ExternalInput").ap()
    wg_d = nc.dram_tensor("wg8", [DM, E], F32, kind="ExternalInput").ap()
    mp_d = nc.dram_tensor("maskprev", [E, 1], F32, kind="ExternalInput").ap()
    tb_d = nc.dram_tensor("tbase", [1], F32, kind="ExternalInput").ap()
    flat_d = nc.dram_tensor("flat_o", [T, 1], I32, kind="ExternalOutput").ap()
    gk_d = nc.dram_tensor("gk_o", [T, 1], F32, kind="ExternalOutput").ap()
    inv_d = nc.dram_tensor("inv_o", [MOE_T + 1, 1], F32, kind="ExternalOutput").ap()
    cnt_i = nc.dram_tensor("cnt_i", [E, 1], F32, kind="Internal").ap()
    cag_i = nc.dram_tensor("cag_i", [NCORES * E, 1], F32, kind="Internal").ap()
    inv_i = nc.dram_tensor("inv_i", [MOE_T + 1, 1], F32, kind="Internal").ap()
    invr_i = nc.dram_tensor("invr_i", [MOE_T + 1, 1], F32, kind="Internal").ap()

    st = ExitStack()
    with st:
        csts_sb = st.enter_context(nc.sbuf_tensor("csts_sb", [128, 8], F32))
        ones1_sb = st.enter_context(nc.sbuf_tensor("ones1_sb", [1, 128], F32))
        ones128_sb = st.enter_context(nc.sbuf_tensor("ones128_sb", [128, 1], F32))
        io8_sb = st.enter_context(nc.sbuf_tensor("io8_sb", [128, 8], F32))
        iorow_sb = st.enter_context(nc.sbuf_tensor("iorow_sb", [8, 8], F32))
        iocol_sb = st.enter_context(nc.sbuf_tensor("iocol_sb", [8, 1], F32))
        id8_sb = st.enter_context(nc.sbuf_tensor("id8_sb", [8, 8], F32))
        io128r_sb = st.enter_context(nc.sbuf_tensor("io128r_sb", [128, 128], F32))
        io128c_sb = st.enter_context(nc.sbuf_tensor("io128c_sb", [128, 1], F32))
        u128_sb = st.enter_context(nc.sbuf_tensor("u128_sb", [128, 128], F32))
        wg_sb = st.enter_context(nc.sbuf_tensor("wg_sb", [128, 16 * E], BF16))
        x3_sb = st.enter_context(nc.sbuf_tensor("x3_sb", [128, 2 * T], BF16))
        lg_sb = st.enter_context(nc.sbuf_tensor("lg_sb", [E, T], F32))
        lgt_sb = st.enter_context(nc.sbuf_tensor("lgt_sb", [128, NTT * E], F32))
        m8_sb = st.enter_context(nc.sbuf_tensor("m8_sb", [128, 8], F32))
        mi_sb = st.enter_context(nc.sbuf_tensor("mi_sb", [128, 8], U32))
        eid_sb = st.enter_context(nc.sbuf_tensor("eid_sb", [128, NTT], F32))
        oh_sb = st.enter_context(nc.sbuf_tensor("oh_sb", [128, NTT * E], F32))
        gate_sb = st.enter_context(nc.sbuf_tensor("gate_sb", [128, NTT], F32))
        nl_sb = st.enter_context(nc.sbuf_tensor("nl_sb", [128, 8], F32))
        ex_sb = st.enter_context(nc.sbuf_tensor("ex_sb", [128, 8], F32))
        sume_sb = st.enter_context(nc.sbuf_tensor("sume_sb", [128, 1], F32))
        cnt_sb = st.enter_context(nc.sbuf_tensor("cnt_sb", [1, E], F32))
        mp_sb = st.enter_context(nc.sbuf_tensor("mp_sb", [E, 1], F32))
        cag_sb = st.enter_context(nc.sbuf_tensor("cag_sb", [8, E], F32))
        run_sb = st.enter_context(nc.sbuf_tensor("run_sb", [1, E], F32))
        runbc_sb = st.enter_context(nc.sbuf_tensor("runbc_sb", [128, E], F32))
        pos_sb = st.enter_context(nc.sbuf_tensor("pos_sb", [128, E], F32))
        posc_sb = st.enter_context(nc.sbuf_tensor("posc_sb", [128, 1], F32))
        keep_sb = st.enter_context(nc.sbuf_tensor("keep_sb", [128, 1], F32))
        flat0_sb = st.enter_context(nc.sbuf_tensor("flat0_sb", [128, 1], F32))
        flat_sb = st.enter_context(nc.sbuf_tensor("flat_sb", [128, NTT], F32))
        flati_sb = st.enter_context(nc.sbuf_tensor("flati_sb", [128, NTT], I32))
        big_sb = st.enter_context(nc.sbuf_tensor("big_sb", [128, 1], F32))
        gk_sb = st.enter_context(nc.sbuf_tensor("gk_sb", [128, NTT], F32))
        tb_sb = st.enter_context(nc.sbuf_tensor("tb_sb", [128, 1], F32))
        tv_sb = st.enter_context(nc.sbuf_tensor("tv_sb", [128, NTT], F32))
        z_sb = st.enter_context(nc.sbuf_tensor("z_sb", [128, 129], F32))

        pL = st.enter_context(nc.psum_tensor("pL", [E, T], F32))
        pT = st.enter_context(nc.psum_tensor("pT", [128, 128], F32))
        pC = st.enter_context(nc.psum_tensor("pC", [128, 8], F32))

        b = B(nc, st)
        sPE, sACT, sDVE, sPOOL = b.sem("sPE"), b.sem("sACT"), b.sem("sDVE"), b.sem("sPOOL")
        sX3 = b.sems(2, "sX3"); sWG = b.sem("sWG"); sMISC = b.sem("sMISC")
        sZ = b.sem("sZ"); sCC = b.sem("sCC"); sSCT = b.sem("sSCT"); sOUT = b.sem("sOUT")
        block = st.enter_context(nc.Block())

        r_pL = Res(); r_pT = Res(); r_pC = Res()
        r_x3 = [Res(), Res()]

        v_o1 = b.op("gpsimd", lambda g: g.memset(ones1_sb[:, :], 1.0), inc=sPOOL)
        v_o128 = b.op("gpsimd", lambda g: g.memset(ones128_sb[:, :], 1.0), inc=sPOOL)
        v_big = b.op("gpsimd", lambda g: g.memset(big_sb[:, :], float(MOE_T)), inc=sPOOL)
        v_i8 = b.op("gpsimd", lambda g: g.iota(io8_sb[:, :], pattern=[[1, 8]], channel_multiplier=0, allow_small_or_imprecise_dtypes=True), inc=sPOOL)
        v_ir = b.op("gpsimd", lambda g: g.iota(iorow_sb[:, :], pattern=[[1, 8]], channel_multiplier=0, allow_small_or_imprecise_dtypes=True), inc=sPOOL)
        v_ic = b.op("gpsimd", lambda g: g.iota(iocol_sb[:, :], pattern=[[0, 1]], channel_multiplier=1, allow_small_or_imprecise_dtypes=True), inc=sPOOL)
        v_id8 = b.op("vector", lambda v: v.tensor_scalar(id8_sb[:, :], iorow_sb[:, :], iocol_sb[:, 0:1], None, ALU.is_equal),
                     inc=sDVE, waits=[v_ir, v_ic])
        v_irr = b.op("gpsimd", lambda g: g.iota(io128r_sb[:, :], pattern=[[1, 128]], channel_multiplier=0, allow_small_or_imprecise_dtypes=True), inc=sPOOL)
        v_icc = b.op("gpsimd", lambda g: g.iota(io128c_sb[:, :], pattern=[[0, 1]], channel_multiplier=1, allow_small_or_imprecise_dtypes=True), inc=sPOOL)
        v_u128 = b.op("vector", lambda v: v.tensor_scalar(u128_sb[:, :], io128r_sb[:, :], io128c_sb[:, 0:1], None, ALU.is_ge),
                      inc=sDVE, waits=[v_irr, v_icc])
        v_zz = b.op("gpsimd", lambda g: g.memset(z_sb[:, :], 0.0), inc=sPOOL)

        wl = b.op("gpsimd", lambda g: g.dma_start(out=wg_sb[:, :].rearrange("p (k e) -> p k e", k=16),
                                                  in_=wg_d.rearrange("(k p) e -> p k e", p=128)),
                  inc=sWG, inc_k=16)
        ml = b.op("sync", lambda s: s.dma_start(out=mp_sb[:, :], in_=mp_d[:, :]), inc=sMISC, inc_k=16)
        tl = b.op("gpsimd", lambda g: g.dma_start(out=tb_sb[:, :], in_=tb_d.unsqueeze(0).partition_broadcast(128)),
                  inc=sMISC, inc_k=16)
        v_lg = None
        for j in range(NCH):
            for k in range(16):
                kpar = k & 1
                x3l = b.op("gpsimd", lambda g, k=k, j=j, kpar=kpar: g.dma_start(
                    out=x3_sb[:, kpar * T + j * NC:kpar * T + j * NC + NC], in_=x3_d[k * 128:(k + 1) * 128, j * NC:(j + 1) * NC]),
                    inc=sX3[kpar], inc_k=16, waits=[r_x3[kpar]])
                v_lg = b.op("tensor", lambda t, k=k, j=j, kpar=kpar: t.matmul(
                    pL[:, j * NC:(j + 1) * NC], wg_sb[:, k * E:(k + 1) * E],
                    x3_sb[:, kpar * T + j * NC:kpar * T + j * NC + NC],
                    start=(k == 0), stop=(k == 15)),
                    inc=sPE, waits=[x3l, wl] + ([r_pL] if k == 0 and j == 0 else []))
                r_x3[kpar].set(v_lg)
        v_lgc = b.op("vector", lambda v: v.tensor_copy(lg_sb[:, :], pL[:, :]), inc=sDVE, waits=[v_lg])
        r_pL.set(v_lgc)

        v_cnt = None
        r_ex = Res(); r_nl = Res()
        for tt in range(NTT):
            t0 = tt * 128
            v_tr = b.op("tensor", lambda t, t0=t0: t.transpose(pT[:, 0:8], lg_sb[:, t0:t0 + 128], id8_sb[:, :]),
                        inc=sPE, waits=[v_lgc, v_id8, r_pT])
            v_lt = b.op("vector", lambda v, tt=tt: v.tensor_copy(lgt_sb[:, tt * E:(tt + 1) * E], pT[:, 0:8]),
                        inc=sDVE, waits=[v_tr])
            r_pT.set(v_lt)
            v_mx = b.op("vector", lambda v, tt=tt: v.max(m8_sb[:, :], lgt_sb[:, tt * E:(tt + 1) * E]),
                        inc=sDVE, waits=[v_lt])
            v_mix = b.op("vector", lambda v, tt=tt: v.max_index(mi_sb[:, :], m8_sb[:, :], lgt_sb[:, tt * E:(tt + 1) * E]),
                         inc=sDVE, waits=[v_mx])
            v_eid = b.op("vector", lambda v, tt=tt: v.tensor_copy(eid_sb[:, tt:tt + 1], mi_sb[:, 0:1]),
                         inc=sDVE, waits=[v_mix])
            v_nl = b.op("vector", lambda v: v.tensor_scalar(nl_sb[:, 0:1], m8_sb[:, 0:1], -1.0, None, ALU.mult),
                        inc=sDVE, waits=[v_mx, r_nl])
            v_ex = b.op("scalar", lambda s, tt=tt: s.activation(out=ex_sb[:, :], in_=lgt_sb[:, tt * E:(tt + 1) * E], func=AF.Exp, bias=nl_sb[:, 0:1]),
                        inc=sACT, waits=[v_nl, v_lt, r_ex])
            r_nl.set(v_ex)
            v_se = b.op("vector", lambda v: v.reduce_sum(sume_sb[:, :], ex_sb[:, :], AX.X),
                        inc=sDVE, waits=[v_ex])
            r_ex.set(v_se)
            v_gt = b.op("vector", lambda v, tt=tt: v.reciprocal(gate_sb[:, tt:tt + 1], sume_sb[:, :]),
                        inc=sDVE, waits=[v_se])
            v_oh = b.op("vector", lambda v, tt=tt: v.tensor_scalar(
                oh_sb[:, tt * E:(tt + 1) * E], io8_sb[:, :], eid_sb[:, tt:tt + 1], None, ALU.is_equal),
                inc=sDVE, waits=[v_eid, v_i8])
            v_cnt = b.op("tensor", lambda t, tt=tt: t.matmul(
                pC[0:1, 0:8], ones128_sb[:, 0:1], oh_sb[:, tt * E:(tt + 1) * E],
                start=(tt == 0), stop=(tt == NTT - 1)),
                inc=sPE, waits=[v_oh, v_o128] + ([r_pC] if tt == 0 else []))
        v_cntc = b.op("vector", lambda v: v.tensor_copy(cnt_sb[:, :], pC[0:1, 0:8]), inc=sDVE, waits=[v_cnt])
        r_pC.set(v_cntc)
        v_cst = b.op("gpsimd", lambda g: g.dma_start(out=cnt_i[:, :], in_=cnt_sb[0:1, :].rearrange("one e -> e one") if False else cnt_sb[0:1, :]),
                     inc=sMISC, inc_k=16, waits=[v_cntc])
        v_ag = b.op("gpsimd", lambda g: g.collective_compute(
            "AllGather", ALU.bypass, replica_groups=[list(range(NCORES))],
            ins=[cnt_i.opt()], outs=[cag_i.opt()]),
            inc=sCC, waits=[v_cst])
        v_cagl = b.op("gpsimd", lambda g: g.dma_start(out=cag_sb[:, :], in_=cag_i.rearrange("(c e) one -> c (e one)", e=E)),
                      inc=sMISC, inc_k=16, waits=[v_ag])
        v_pt = b.op("tensor", lambda t: t.matmul(pC[0:1, 0:8], mp_sb[:, :], cag_sb[:, :], start=True, stop=True),
                    inc=sPE, waits=[v_cagl, ml, r_pC])
        v_run0 = b.op("vector", lambda v: v.tensor_copy(run_sb[:, :], pC[0:1, 0:8]), inc=sDVE, waits=[v_pt])
        r_pC.set(v_run0)

        b.op("gpsimd", lambda g: g.dma_start(out=inv_i[0:MOE_T, :].rearrange("(a b) one -> a (b one)", a=128), in_=z_sb[:, 0:128]),
             inc=sZ, inc_k=16, waits=[v_zz])
        b.op("gpsimd", lambda g: g.dma_start(out=inv_i[MOE_T:MOE_T + 1, :], in_=z_sb[0:1, 128:129]),
             inc=sZ, inc_k=16, waits=[v_zz])

        v_run = v_run0
        for tt in range(NTT):
            t0 = tt * 128
            v_rb = b.op("tensor", lambda t: t.matmul(pT[:, 0:8], ones1_sb[0:1, :], run_sb[0:1, :], start=True, stop=True),
                        inc=sPE, waits=[v_run, v_o1, r_pT])
            v_rbc = b.op("vector", lambda v: v.tensor_copy(runbc_sb[:, :], pT[:, 0:8]), inc=sDVE, waits=[v_rb])
            r_pT.set(v_rbc)
            v_cs = b.op("tensor", lambda t, tt=tt: t.matmul(pC[:, 0:8], u128_sb[:, :], oh_sb[:, tt * E:(tt + 1) * E], start=True, stop=True),
                        inc=sPE, waits=[v_u128, r_pC])
            v_p1 = b.op("vector", lambda v: v.tensor_add(pos_sb[:, :], pC[:, 0:8], runbc_sb[:, :]),
                        inc=sDVE, waits=[v_cs, v_rbc])
            r_pC.set(v_p1)
            v_p2 = b.op("vector", lambda v: v.tensor_scalar(pos_sb[:, :], pos_sb[:, :], -1.0, None, ALU.add),
                        inc=sDVE, waits=[v_p1])
            v_pm = b.op("vector", lambda v, tt=tt: v.tensor_mul(pos_sb[:, :], pos_sb[:, :], oh_sb[:, tt * E:(tt + 1) * E]),
                        inc=sDVE, waits=[v_p2])
            v_pc = b.op("vector", lambda v: v.reduce_sum(posc_sb[:, :], pos_sb[:, :], AX.X),
                        inc=sDVE, waits=[v_pm])
            v_tt = b.op("tensor", lambda t, tt=tt: t.matmul(pC[0:1, 0:8], ones128_sb[:, 0:1], oh_sb[:, tt * E:(tt + 1) * E], start=True, stop=True),
                        inc=sPE, waits=[v_pm, r_pC])
            v_run = b.op("vector", lambda v: v.tensor_add(run_sb[0:1, :], run_sb[0:1, :], pC[0:1, 0:8]),
                         inc=sDVE, waits=[v_tt, v_rb])
            r_pC.set(v_run)
            v_kp = b.op("vector", lambda v: v.tensor_scalar(keep_sb[:, :], posc_sb[:, :], float(MOE_C), None, ALU.is_lt),
                        inc=sDVE, waits=[v_pc])
            v_f0 = b.op("vector", lambda v, tt=tt: v.tensor_scalar(flat0_sb[:, :], eid_sb[:, tt:tt + 1], float(MOE_C), None, ALU.mult),
                        inc=sDVE, waits=[v_kp])
            v_f1 = b.op("vector", lambda v: v.tensor_add(flat0_sb[:, :], flat0_sb[:, :], posc_sb[:, :]),
                        inc=sDVE, waits=[v_f0])
            v_fa = b.op("vector", lambda v: v.tensor_scalar(flat0_sb[:, :], flat0_sb[:, :], -float(MOE_T), None, ALU.add),
                        inc=sDVE, waits=[v_f1])
            v_fb = b.op("vector", lambda v: v.tensor_mul(flat0_sb[:, :], flat0_sb[:, :], keep_sb[:, :]),
                        inc=sDVE, waits=[v_fa])
            v_fl = b.op("vector", lambda v, tt=tt: v.tensor_scalar(flat_sb[:, tt:tt + 1], flat0_sb[:, :], float(MOE_T), None, ALU.add),
                        inc=sDVE, waits=[v_fb])
            v_fi = b.op("vector", lambda v, tt=tt: v.tensor_copy(flati_sb[:, tt:tt + 1], flat_sb[:, tt:tt + 1]),
                        inc=sDVE, waits=[v_fl])
            v_gk = b.op("vector", lambda v, tt=tt: v.tensor_mul(gk_sb[:, tt:tt + 1], gate_sb[:, tt:tt + 1], keep_sb[:, :]),
                        inc=sDVE, waits=[v_kp, v_gt])
            v_tv = b.op("vector", lambda v, tt=tt, t0=t0: v.tensor_scalar(
                tv_sb[:, tt:tt + 1], io128c_sb[:, 0:1], float(t0 + 1), tb_sb[:, 0:1], ALU.add, ALU.add),
                inc=sDVE, waits=[v_icc, tl])
            v_sc = b.op("gpsimd", lambda g, tt=tt: g.indirect_dma_start(
                out=inv_i[:, :], out_offset=bass.IndirectOffsetOnAxis(flati_sb[:, tt:tt + 1], 0),
                in_=tv_sb[:, tt:tt + 1], in_offset=None,
                bounds_check=MOE_T - 1, oob_is_err=False),
                inc=sSCT, inc_k=16, waits=[v_fi, v_tv, sZ.total()])
        v_ar = b.op("gpsimd", lambda g: g.collective_compute(
            "AllReduce", ALU.max, replica_groups=[list(range(NCORES))],
            ins=[inv_i.opt()], outs=[invr_i.opt()]),
            inc=sCC, waits=[sSCT.total()])
        b.op("gpsimd", lambda g: g.dma_start(out=inv_d[:, :], in_=invr_i[:, :]), inc=sOUT, inc_k=16, waits=[v_ar])
        for tt in range(NTT):
            b.op("gpsimd", lambda g, tt=tt: g.dma_start(out=flat_d[tt * 128:(tt + 1) * 128, :], in_=flati_sb[:, tt:tt + 1]),
                 inc=sOUT, inc_k=16, waits=[v_fi])
            b.op("gpsimd", lambda g, tt=tt: g.dma_start(out=gk_d[tt * 128:(tt + 1) * 128, :], in_=gk_sb[:, tt:tt + 1]),
                 inc=sOUT, inc_k=16, waits=[v_gk])
        b.op("gpsimd", lambda g: g.engine_nop(), waits=[sOUT.total()])
        b.finalize(block)
    return nc



def build_p8():
    nc = bass.Bass("TRN2", target_bir_lowering=False)
    T = TPC
    NC = 512
    NCH = T // NC
    NM = DM // 128
    NO = MOE_H // 128
    NTT = T // 128

    x3r_d = nc.dram_tensor("x3r", [MOE_T, DM], BF16, kind="ExternalInput").ap()
    inv_d = nc.dram_tensor("inv_s", [T, 1], F32, kind="ExternalInput").ap()
    w1_d = nc.dram_tensor("w1e", [DM, MOE_H], F32, kind="ExternalInput").ap()
    w3_d = nc.dram_tensor("w3e", [DM, MOE_H], F32, kind="ExternalInput").ap()
    w2_d = nc.dram_tensor("w2e", [MOE_H, DM], F32, kind="ExternalInput").ap()
    x3_d = nc.dram_tensor("x3", [DM, T], BF16, kind="ExternalInput").ap()
    sw1_d = nc.dram_tensor("sw1", [DM, MOE_H], F32, kind="ExternalInput").ap()
    sw3_d = nc.dram_tensor("sw3", [DM, MOE_H], F32, kind="ExternalInput").ap()
    sw2_d = nc.dram_tensor("sw2", [MOE_H, DM], F32, kind="ExternalInput").ap()
    oe_d = nc.dram_tensor("oe_r", [T, DM], BF16, kind="ExternalOutput").ap()
    sh_d = nc.dram_tensor("sh_o", [DM, T], BF16, kind="ExternalOutput").ap()

    st = ExitStack()
    stS = ExitStack()
    with st:
        ident_sb = st.enter_context(nc.sbuf_tensor("ident_sb", [128, 128], BF16))
        ior_sb = st.enter_context(nc.sbuf_tensor("ior_sb", [128, 128], F32))
        ioc_sb = st.enter_context(nc.sbuf_tensor("ioc_sb", [128, 1], F32))
        h_sb = st.enter_context(nc.sbuf_tensor("h_sb", [128, NO * T], BF16))
        wga_sb = st.enter_context(nc.sbuf_tensor("wga_sb", [128, 2 * 2048], BF16))
        wgb_sb = st.enter_context(nc.sbuf_tensor("wgb_sb", [128, 2 * 2048], BF16))
        wd_sb = st.enter_context(nc.sbuf_tensor("wd_sb", [128, 2 * 128], BF16))
        sg_sb = st.enter_context(nc.sbuf_tensor("sg_sb", [128, T], F32))
        ob_sb = st.enter_context(nc.sbuf_tensor("ob_sb", [128, 2 * T], BF16))
        x3_sb = stS.enter_context(nc.sbuf_tensor("x3_sb", [128, 16 * T], BF16))

        pG = st.enter_context(nc.psum_tensor("pG", [128, T], F32))
        pU = st.enter_context(nc.psum_tensor("pU", [128, T], F32))
        pT_ = pU[:, 0:64].bitcast(BF16)

        b = B(nc, st)
        sPE, sACT, sDVE, sPOOL = b.sem("sPE"), b.sem("sACT"), b.sem("sDVE"), b.sem("sPOOL")
        sX3 = b.sem("sX3")
        sWA = b.sems(2, "sWA"); sWB = b.sems(2, "sWB"); sWD = b.sems(2, "sWD")
        sSH = b.sems(2, "sSH"); sOE = b.sems(2, "sOE")
        sIV = b.sem("sIV"); sGR = b.sems(2, "sGR")
        block = st.enter_context(nc.Block())

        r_pG = Res(); r_pU = Res(); r_pT = r_pU
        r_wa = [Res(), Res()]; r_wb = [Res(), Res()]; r_wd = [Res(), Res()]
        r_sg = Res(); r_h = Res(); r_ob = [Res(), Res()]

        v_ir = b.op("gpsimd", lambda g: g.iota(ior_sb[:, :], pattern=[[1, 128]], channel_multiplier=0, allow_small_or_imprecise_dtypes=True), inc=sPOOL)
        v_ic = b.op("gpsimd", lambda g: g.iota(ioc_sb[:, :], pattern=[[0, 1]], channel_multiplier=1, allow_small_or_imprecise_dtypes=True), inc=sPOOL)
        v_id = b.op("vector", lambda v: v.tensor_scalar(ident_sb[:, :], ior_sb[:, :], ioc_sb[:, 0:1], None, ALU.is_equal),
                    inc=sDVE, waits=[v_ir, v_ic])

        def ffn(src_tiles_ap, src_dep, wg1_d, wg3_d, wdn_d, out_cb, tag):
            v_h = None
            for o in range(NO):
                par = o & 1
                o0 = o * 128
                for k in range(16):
                    ga = b.op("gpsimd", lambda g, k=k, o0=o0, par=par, wg1_d=wg1_d: g.dma_start(
                        out=wga_sb[:, par * 2048 + k * 128:par * 2048 + (k + 1) * 128],
                        in_=wg1_d[k * 128:(k + 1) * 128, o0:o0 + 128]),
                        inc=sWA[par], inc_k=16, waits=[r_wa[par]] if k == 0 else [])
                ga = sWA[par].total()
                for k in range(16):
                    gb = b.op("gpsimd", lambda g, k=k, o0=o0, par=par, wg3_d=wg3_d: g.dma_start(
                        out=wgb_sb[:, par * 2048 + k * 128:par * 2048 + (k + 1) * 128],
                        in_=wg3_d[k * 128:(k + 1) * 128, o0:o0 + 128]),
                        inc=sWB[par], inc_k=16, waits=[r_wb[par]] if k == 0 else [])
                gb = sWB[par].total()
                v_gm = None
                for j in range(NCH):
                    for k in range(16):
                        v_gm = b.op("tensor", lambda t, k=k, j=j, par=par: t.matmul(
                            pG[:, j * NC:(j + 1) * NC], wga_sb[:, par * 2048 + k * 128:par * 2048 + (k + 1) * 128],
                            src_tiles_ap(k)[:, j * NC:(j + 1) * NC], start=(k == 0), stop=(k == 15)),
                            inc=sPE if k == 15 else None,
                            waits=([ga, src_dep, r_pG] if j == 0 and k == 0 else []))
                v_um = None
                for j in range(NCH):
                    for k in range(16):
                        v_um = b.op("tensor", lambda t, k=k, j=j, par=par: t.matmul(
                            pU[:, j * NC:(j + 1) * NC], wgb_sb[:, par * 2048 + k * 128:par * 2048 + (k + 1) * 128],
                            src_tiles_ap(k)[:, j * NC:(j + 1) * NC], start=(k == 0), stop=(k == 15)),
                            inc=sPE if k == 15 else None,
                            waits=([gb, r_pU] if j == 0 and k == 0 else []))
                r_wa[par].set(v_gm)
                r_wb[par].set(v_um)
                v_sg = b.op("scalar", lambda s: s.activation(out=sg_sb[:, :], in_=pG[:, :], func=AF.Silu),
                            inc=sACT, waits=[v_gm, r_sg])
                r_pG.set(v_sg)
                v_h = b.op("vector", lambda v, o=o: v.tensor_mul(h_sb[:, o * T:(o + 1) * T], sg_sb[:, :], pU[:, :]),
                           inc=sDVE, waits=[v_sg, v_um, r_h])
                r_pU.set(v_h)
                r_sg.set(v_h)
            v_h_all = v_h
            v_dm_last = None
            for m in range(NM):
                par = m & 1
                m0 = m * 128
                v_dm = None
                for o in range(NO):
                    wpar = o & 1
                    wl = b.op("gpsimd", lambda g, o=o, m0=m0, wpar=wpar, wdn_d=wdn_d: g.dma_start(
                        out=wd_sb[:, wpar * 128:(wpar + 1) * 128], in_=wdn_d[o * 128:(o + 1) * 128, m0:m0 + 128]),
                        inc=sWD[wpar], inc_k=16, waits=[r_wd[wpar]])
                    for j in range(NCH):
                        v_dm = b.op("tensor", lambda t, o=o, j=j, wpar=wpar: t.matmul(
                            pG[:, j * NC:(j + 1) * NC], wd_sb[:, wpar * 128:(wpar + 1) * 128],
                            h_sb[:, o * T + j * NC:o * T + (j + 1) * NC],
                            start=(o == 0), stop=(o == NO - 1)),
                            inc=sPE if j == NCH - 1 else None,
                            waits=([wl] + ([v_h_all, r_pG] if o == 0 else [])))
                    r_wd[wpar].set(v_dm)
                out_cb(m, v_dm)
                v_dm_last = v_dm
            r_h.set(v_dm_last)
            return v_dm_last

        for k in range(16):
            b.op("gpsimd", lambda g, k=k: g.dma_start(out=x3_sb[:, k * T:(k + 1) * T], in_=x3_d[k * 128:(k + 1) * 128, :]),
                 inc=sX3, inc_k=16)
        v_x3 = sX3.total()

        def sh_out(m, v_dm):
            par = m & 1
            m0 = m * 128
            ob = ob_sb[:, par * T:(par + 1) * T]
            v_ev = b.op("vector", lambda v, ob=ob: v.tensor_copy(ob[:, :], pG[:, :]),
                        inc=sDVE, waits=[v_dm, r_ob[par]])
            r_pG.set(v_ev)
            v_st = b.op("gpsimd", lambda g, m0=m0, ob=ob: g.dma_start(out=sh_d[m0:m0 + 128, :], in_=ob[:, :]),
                        inc=sSH[par], inc_k=16, waits=[v_ev])
            r_ob[par].set(v_st)

        ffn(lambda k: x3_sb[:, k * T:(k + 1) * T], v_x3, sw1_d, sw3_d, sw2_d, sh_out, "sh")

        stS.close()
        BARS = [sX3.total(), sSH[0].total(), sSH[1].total(), (sPE, sPE.n), (sDVE, sDVE.n), (sACT, sACT.n)]
        iv_sb = st.enter_context(nc.sbuf_tensor("iv_sb", [128, NTT], F32))
        ivi_sb = st.enter_context(nc.sbuf_tensor("ivi_sb", [128, NTT], I32))
        xd_sb = st.enter_context(nc.sbuf_tensor("xd_sb", [128, 2 * DM], BF16))
        xdT_sb = st.enter_context(nc.sbuf_tensor("xdT_sb", [128, 16 * T], BF16))

        for tt in range(NTT):
            ivl = b.op("gpsimd", lambda g, tt=tt: g.dma_start(
                out=iv_sb[:, tt:tt + 1], in_=inv_d[tt * 128:(tt + 1) * 128, :]),
                inc=sIV, inc_k=16, waits=BARS if tt == 0 else [])
        ivl = sIV.total()
        v_iv1 = b.op("vector", lambda v: v.tensor_scalar(iv_sb[:, :], iv_sb[:, :], -1.0, None, ALU.add),
                     inc=sDVE, waits=[ivl] + BARS)
        v_iv2 = b.op("vector", lambda v: v.tensor_copy(ivi_sb[:, :], iv_sb[:, :]),
                     inc=sDVE, waits=[v_iv1])

        r_xd = [Res(), Res()]
        v_tr_last = None
        for tt in range(NTT):
            par = tt & 1
            t0 = tt * 128
            xd = xd_sb[:, par * DM:(par + 1) * DM]
            v_z = b.op("vector", lambda v, xd=xd: v.memset(xd[:, :], 0.0),
                       inc=sDVE, waits=[r_xd[par]] + (BARS if tt < 2 else []))
            v_g = b.op("gpsimd", lambda g, tt=tt, xd=xd: g.indirect_dma_start(
                out=xd[:, :], out_offset=None,
                in_=x3r_d[:, :], in_offset=bass.IndirectOffsetOnAxis(ivi_sb[:, tt:tt + 1], 0),
                bounds_check=MOE_T - 1, oob_is_err=False),
                inc=sGR[par], inc_k=16, waits=[v_z, v_iv2])
            for m in range(NM):
                v_tr = b.op("tensor", lambda t, m=m, xd=xd: t.transpose(
                    pT_[:, :], xd[:, m * 128:(m + 1) * 128], ident_sb[:, :]),
                    inc=sPE, waits=[v_g, v_id, r_pT])
                v_te = b.op("vector", lambda v, m=m, t0=t0: v.tensor_copy(
                    xdT_sb[:, m * T + t0:m * T + t0 + 128], pT_[:, :]),
                    inc=sDVE, waits=[v_tr] + (BARS if tt < 2 and m == 0 else []))
                r_pT.set(v_te)
                v_tr_last = v_te
            r_xd[par].set(v_tr)
        v_xdT = v_tr_last

        oeb_used = [Res(), Res()]

        def oe_out(m, v_dm):
            par = m & 1
            m0 = m * 128
            ob = ob_sb[:, par * T:(par + 1) * T]
            v_ev = b.op("vector", lambda v, ob=ob: v.tensor_copy(ob[:, :], pG[:, :]),
                        inc=sDVE, waits=[v_dm, r_ob[par]])
            r_pG.set(v_ev)
            v_st_last = None
            for tt in range(NTT):
                t0 = tt * 128
                v_tr = b.op("tensor", lambda t, t0=t0, ob=ob: t.transpose(
                    pT_[:, :], ob[:, t0:t0 + 128], ident_sb[:, :]),
                    inc=sPE, waits=[v_ev, r_pT])
                v_te = b.op("vector", lambda v, par=par: v.tensor_copy(xd_sb[:, par * DM:par * DM + 128], pT_[:, :]),
                            inc=sDVE, waits=[v_tr, oeb_used[par]])
                r_pT.set(v_te)
                v_st = b.op("gpsimd", lambda g, t0=t0, m0=m0, par=par: g.dma_start(
                    out=oe_d[t0:t0 + 128, m0:m0 + 128], in_=xd_sb[:, par * DM:par * DM + 128]),
                    inc=sOE[par], inc_k=16, waits=[v_te])
                oeb_used[par].set(v_st)
                v_st_last = v_st
            r_ob[par].set(v_ev, v_st_last)

        ffn(lambda k: xdT_sb[:, k * T:(k + 1) * T], v_xdT, w1_d, w3_d, w2_d, oe_out, "oe")

        b.op("gpsimd", lambda g: g.engine_nop(),
             waits=[sOE[0].total(), sOE[1].total(), sSH[0].total(), sSH[1].total()])
        b.finalize(block)
    return nc



def build_p9():
    nc = bass.Bass("TRN2", target_bir_lowering=False)
    T = TPC
    NC = 512
    NCH = T // NC
    NM = DM // 128
    NTT = T // 128

    oe_d = nc.dram_tensor("oe_f", [MOE_T, DM], BF16, kind="ExternalInput").ap()
    flat_d = nc.dram_tensor("flat_s", [T, 1], I32, kind="ExternalInput").ap()
    gk_d = nc.dram_tensor("gk_s", [T, 1], F32, kind="ExternalInput").ap()
    sh_d = nc.dram_tensor("sh_o", [DM, T], BF16, kind="ExternalInput").ap()
    x3_d = nc.dram_tensor("x3", [DM, T], BF16, kind="ExternalInput").ap()
    x4_d = nc.dram_tensor("x4", [DM, T], F32, kind="ExternalOutput").ap()
    tot_i = nc.dram_tensor("tot_i", [DM, T], F32, kind="Internal").ap()

    st = ExitStack()
    with st:
        csts_sb = st.enter_context(nc.sbuf_tensor("csts_sb", [128, 8], F32))
        ones1_sb = st.enter_context(nc.sbuf_tensor("ones1_sb", [1, 128], F32))
        ones128_sb = st.enter_context(nc.sbuf_tensor("ones128_sb", [128, 1], F32))
        ior_sb = st.enter_context(nc.sbuf_tensor("ior_sb", [128, 128], F32))
        ioc_sb = st.enter_context(nc.sbuf_tensor("ioc_sb", [128, 1], F32))
        ident_sb = st.enter_context(nc.sbuf_tensor("ident_sb", [128, 128], BF16))
        sum_sb = st.enter_context(nc.sbuf_tensor("sum_sb", [1, T], F32))
        ssq_sb = st.enter_context(nc.sbuf_tensor("ssq_sb", [1, T], F32))
        tmp_sb = st.enter_context(nc.sbuf_tensor("tmp_sb", [1, T], F32))
        fl_sb = st.enter_context(nc.sbuf_tensor("fl_sb", [128, NTT], I32))
        gkc_sb = st.enter_context(nc.sbuf_tensor("gkc_sb", [128, NTT], F32))
        mo_sb = st.enter_context(nc.sbuf_tensor("mo_sb", [128, 2 * DM], BF16))
        moT_sb = st.enter_context(nc.sbuf_tensor("moT_sb", [128, 16 * T], BF16))
        mof_sb = st.enter_context(nc.sbuf_tensor("mof_sb", [128, 2 * T], F32))
        sh_sb = st.enter_context(nc.sbuf_tensor("sh_sb", [128, 2 * T], F32))
        x3_sb = st.enter_context(nc.sbuf_tensor("x3_sb", [128, 2 * T], F32))
        sq_sb = st.enter_context(nc.sbuf_tensor("sq_sb", [128, 2 * T], F32))
        mubc_sb = st.enter_context(nc.sbuf_tensor("mubc_sb", [128, T], F32))
        rsbc_sb = st.enter_context(nc.sbuf_tensor("rsbc_sb", [128, T], F32))
        x4_sb = st.enter_context(nc.sbuf_tensor("x4_sb", [128, 2 * T], F32))

        pA = st.enter_context(nc.psum_tensor("pA", [128, T], F32))
        pT_ = st.enter_context(nc.psum_tensor("pT_", [128, 128], BF16))

        b = B(nc, st)
        sPE, sACT, sDVE, sPOOL = b.sem("sPE"), b.sem("sACT"), b.sem("sDVE"), b.sem("sPOOL")
        sFL = b.sem("sFL"); sGK = b.sem("sGK")
        sGR = b.sems(2, "sGR"); sSH = b.sems(2, "sSH"); sX3 = b.sems(2, "sX3")
        sTOT = b.sems(2, "sTOT"); sTOTL = b.sems(2, "sTOTL"); sX4 = b.sems(2, "sX4")
        block = st.enter_context(nc.Block())

        _cdeps = setup_consts(nc, csts_sb, b, sPOOL, [1e-5, 1.0 / DM, -0.5])
        v_o1 = b.op("gpsimd", lambda g: g.memset(ones1_sb[:, :], 1.0), inc=sPOOL)
        v_o128 = b.op("gpsimd", lambda g: g.memset(ones128_sb[:, :], 1.0), inc=sPOOL)
        v_zs = b.op("gpsimd", lambda g: g.memset(sum_sb[:, :], 0.0), inc=sPOOL)
        v_zq = b.op("gpsimd", lambda g: g.memset(ssq_sb[:, :], 0.0), inc=sPOOL)
        v_ir = b.op("gpsimd", lambda g: g.iota(ior_sb[:, :], pattern=[[1, 128]], channel_multiplier=0, allow_small_or_imprecise_dtypes=True), inc=sPOOL)
        v_ic = b.op("gpsimd", lambda g: g.iota(ioc_sb[:, :], pattern=[[0, 1]], channel_multiplier=1, allow_small_or_imprecise_dtypes=True), inc=sPOOL)
        v_id = b.op("vector", lambda v: v.tensor_scalar(ident_sb[:, :], ior_sb[:, :], ioc_sb[:, 0:1], None, ALU.is_equal),
                    inc=sDVE, waits=[v_ir, v_ic])

        r_pA = Res(); r_pT = Res()
        r_mo = [Res(), Res()]
        r_mof = [Res(), Res()]; r_sh = [Res(), Res()]; r_x3 = [Res(), Res()]; r_sq = [Res(), Res()]
        r_x4 = [Res(), Res()]; r_xt = [Res(), Res()]

        for tt in range(NTT):
            b.op("gpsimd", lambda g, tt=tt: g.dma_start(out=fl_sb[:, tt:tt + 1], in_=flat_d[tt * 128:(tt + 1) * 128, :]),
                 inc=sFL, inc_k=16)
            b.op("gpsimd", lambda g, tt=tt: g.dma_start(out=gkc_sb[:, tt:tt + 1], in_=gk_d[tt * 128:(tt + 1) * 128, :]),
                 inc=sGK, inc_k=16)
        v_fl = sFL.total()
        v_gk = sGK.total()

        v_te_last = None
        for tt in range(NTT):
            par = tt & 1
            t0 = tt * 128
            mo = mo_sb[:, par * DM:(par + 1) * DM]
            v_z = b.op("vector", lambda v, mo=mo: v.memset(mo[:, :], 0.0),
                       inc=sDVE, waits=[r_mo[par]])
            v_g = b.op("gpsimd", lambda g, tt=tt, mo=mo: g.indirect_dma_start(
                out=mo[:, :], out_offset=None,
                in_=oe_d[:, :], in_offset=bass.IndirectOffsetOnAxis(fl_sb[:, tt:tt + 1], 0),
                bounds_check=MOE_T - 1, oob_is_err=False),
                inc=sGR[par], inc_k=16, waits=[v_z, v_fl])
            v_gm = b.op("vector", lambda v, tt=tt, mo=mo: v.tensor_scalar(
                mo[:, :], mo[:, :], gkc_sb[:, tt:tt + 1], None, ALU.mult),
                inc=sDVE, waits=[v_g, v_gk])
            for m in range(NM):
                v_tr = b.op("tensor", lambda t, m=m, mo=mo: t.transpose(
                    pT_[:, :], mo[:, m * 128:(m + 1) * 128], ident_sb[:, :]),
                    inc=sPE, waits=[v_gm, v_id, r_pT])
                v_te = b.op("vector", lambda v, m=m, t0=t0: v.tensor_copy(
                    moT_sb[:, m * T + t0:m * T + t0 + 128], pT_[:, :]),
                    inc=sDVE, waits=[v_tr])
                r_pT.set(v_te)
                v_te_last = v_te
            r_mo[par].set(v_tr)
        v_moT = v_te_last

        v_a2 = None
        for m in range(NM):
            par = m & 1
            m0 = m * 128
            shl = b.op("gpsimd", lambda g, m0=m0, par=par: g.dma_start(
                out=sh_sb[:, par * T:(par + 1) * T], in_=sh_d[m0:m0 + 128, :]),
                inc=sSH[par], inc_k=16, waits=[r_sh[par]])
            x3l = b.op("gpsimd", lambda g, m0=m0, par=par: g.dma_start(
                out=x3_sb[:, par * T:(par + 1) * T], in_=x3_d[m0:m0 + 128, :]),
                inc=sX3[par], inc_k=16, waits=[r_x3[par]])
            mof = mof_sb[:, par * T:(par + 1) * T]
            sh = sh_sb[:, par * T:(par + 1) * T]
            x3t = x3_sb[:, par * T:(par + 1) * T]
            sq = sq_sb[:, par * T:(par + 1) * T]
            v_c1 = b.op("vector", lambda v, m=m, mof=mof: v.tensor_copy(mof[:, :], moT_sb[:, m * T:(m + 1) * T]),
                        inc=sDVE, waits=[v_moT, r_mof[par]])
            v_c2 = b.op("vector", lambda v, mof=mof, sh=sh: v.tensor_add(mof[:, :], mof[:, :], sh[:, :]),
                        inc=sDVE, waits=[v_c1, shl])
            r_sh[par].set(v_c2)
            v_c3 = b.op("vector", lambda v, mof=mof, x3t=x3t: v.tensor_add(mof[:, :], mof[:, :], x3t[:, :]),
                        inc=sDVE, waits=[v_c2, x3l])
            r_x3[par].set(v_c3)
            v_ts = b.op("gpsimd", lambda g, m0=m0, mof=mof: g.dma_start(out=tot_i[m0:m0 + 128, :], in_=mof[:, :]),
                        inc=sTOT[par], inc_k=16, waits=[v_c3])
            v_sq = b.op("scalar", lambda s, mof=mof, sq=sq: s.activation(out=sq[:, :], in_=mof[:, :], func=AF.Square),
                        inc=sACT, waits=[v_c3, r_sq[par]])
            r_mof[par].set(v_sq, v_ts)
            for j in range(NCH):
                v_s1 = b.op("tensor", lambda t, j=j, mof=mof: t.matmul(
                    pA[0:1, j * NC:(j + 1) * NC], ones128_sb[:, 0:1], mof[:, j * NC:(j + 1) * NC], start=True, stop=True),
                    inc=sPE, waits=[v_c3, v_o128, r_pA])
                v_a1 = b.op("vector", lambda v, j=j: v.tensor_add(sum_sb[0:1, j * NC:(j + 1) * NC], sum_sb[0:1, j * NC:(j + 1) * NC], pA[0:1, j * NC:(j + 1) * NC]),
                            inc=sDVE, waits=[v_s1, v_zs])
                v_s2 = b.op("tensor", lambda t, j=j, sq=sq: t.matmul(
                    pA[0:1, j * NC:(j + 1) * NC], ones128_sb[:, 0:1], sq[:, j * NC:(j + 1) * NC], start=True, stop=True),
                    inc=sPE, waits=[v_sq, v_a1])
                v_a2 = b.op("vector", lambda v, j=j: v.tensor_add(ssq_sb[0:1, j * NC:(j + 1) * NC], ssq_sb[0:1, j * NC:(j + 1) * NC], pA[0:1, j * NC:(j + 1) * NC]),
                            inc=sDVE, waits=[v_s2, v_zq])
                r_pA.set(v_a2)
            r_sq[par].set(v_s2)
        v_stats = v_a2

        v_mu = b.op("vector", lambda v: v.tensor_scalar(sum_sb[0:1, :], sum_sb[0:1, :], 1.0 / DM, None, ALU.mult),
                    inc=sDVE, waits=[v_stats])
        v_m2 = b.op("vector", lambda v: v.tensor_mul(tmp_sb[0:1, :], sum_sb[0:1, :], sum_sb[0:1, :]),
                    inc=sDVE, waits=[v_mu])
        v_var = b.op("vector", lambda v: v.scalar_tensor_tensor(ssq_sb[0:1, :], ssq_sb[0:1, :], 1.0 / DM, tmp_sb[0:1, :], ALU.mult, ALU.subtract),
                     inc=sDVE, waits=[v_m2])
        v_ln = b.op("scalar", lambda s: s.activation(out=ssq_sb[0:1, :], in_=ssq_sb[0:1, :], func=AF.Ln, bias=1e-5),
                    inc=sACT, waits=[v_var])
        v_rs = b.op("scalar", lambda s: s.activation(out=ssq_sb[0:1, :], in_=ssq_sb[0:1, :], func=AF.Exp, scale=-0.5),
                    inc=sACT, waits=[v_ln])
        v_mbc = None
        for j in range(NCH):
            v_bm = b.op("tensor", lambda t, j=j: t.matmul(
                pA[:, j * NC:(j + 1) * NC], ones1_sb[0:1, :], sum_sb[0:1, j * NC:(j + 1) * NC], start=True, stop=True),
                inc=sPE, waits=[v_mu, v_o1, r_pA])
            v_mbc = b.op("vector", lambda v, j=j: v.tensor_copy(mubc_sb[:, j * NC:(j + 1) * NC], pA[:, j * NC:(j + 1) * NC]),
                         inc=sDVE, waits=[v_bm])
            r_pA.set(v_mbc)
        v_rbc = None
        for j in range(NCH):
            v_rm = b.op("tensor", lambda t, j=j: t.matmul(
                pA[:, j * NC:(j + 1) * NC], ones1_sb[0:1, :], ssq_sb[0:1, j * NC:(j + 1) * NC], start=True, stop=True),
                inc=sPE, waits=[v_rs, v_o1, r_pA])
            v_rbc = b.op("vector", lambda v, j=j: v.tensor_copy(rsbc_sb[:, j * NC:(j + 1) * NC], pA[:, j * NC:(j + 1) * NC]),
                         inc=sDVE, waits=[v_rm])
            r_pA.set(v_rbc)
        for m in range(NM):
            par = m & 1
            m0 = m * 128
            tl = b.op("gpsimd", lambda g, m0=m0, par=par: g.dma_start(
                out=x4_sb[:, par * T:(par + 1) * T], in_=tot_i[m0:m0 + 128, :]),
                inc=sTOTL[par], inc_k=16, waits=[r_xt[par], sTOT[0].total(), sTOT[1].total()])
            x4 = x4_sb[:, par * T:(par + 1) * T]
            v_n1 = b.op("vector", lambda v, x4=x4: v.tensor_sub(x4[:, :], x4[:, :], mubc_sb[:, :]),
                        inc=sDVE, waits=[tl, v_mbc])
            v_n2 = b.op("vector", lambda v, x4=x4: v.tensor_mul(x4[:, :], x4[:, :], rsbc_sb[:, :]),
                        inc=sDVE, waits=[v_n1, v_rbc])
            v_st = b.op("gpsimd", lambda g, m0=m0, x4=x4: g.dma_start(out=x4_d[m0:m0 + 128, :], in_=x4[:, :]),
                        inc=sX4[par], inc_k=16, waits=[v_n2])
            r_xt[par].set(v_st, v_n2)

        b.op("gpsimd", lambda g: g.engine_nop(), waits=[sX4[0].total(), sX4[1].total()])
        b.finalize(block)
    return nc



def _run(nc, maps):
    return run_bass_kernel_spmd(nc, maps, core_ids=list(range(NCORES))).results


def kernel(**ins):
    x = np.asarray(ins["L0_attn_x"])
    xtb = [np.ascontiguousarray(x[0].T), np.ascontiguousarray(x[1].T)]

    conv_w = np.ascontiguousarray(ins["L0_attn_conv_w"].reshape(DIN, 4))
    maps = []
    for i in range(NCORES):
        bsel, s0 = i // 4, (i % 4) * TPC
        ext = np.zeros((DM, TPC + 3), np.float32)
        if s0 == 0:
            ext[:, 3:] = xtb[bsel][:, 0:TPC]
        else:
            ext[:, :] = xtb[bsel][:, s0 - 3:s0 + TPC]
        maps.append({
            "xT": ext, "w_in": ins["L0_attn_in_proj_w"], "conv_w": conv_w,
            "conv_b": ins["L0_attn_conv_b"], "w_xp": ins["L0_attn_x_proj_w"],
            "w_dt": ins["L0_attn_dt_proj_w"], "b_dt": ins["L0_attn_dt_proj_b"],
        })
    r1 = _run(build_p1(), maps)

    dt_cm = np.concatenate([np.asarray(r["dt"]) for r in r1], axis=1)
    dtu_cm = np.concatenate([np.asarray(r["dtu"]) for r in r1], axis=1)
    bn_f = np.concatenate([np.asarray(r["bn"]) for r in r1], axis=1)
    cn_f = np.concatenate([np.asarray(r["cn"]) for r in r1], axis=1)
    alog = np.asarray(ins["L0_attn_A_log"]).reshape(-1)
    maps = []
    for i in range(NCORES):
        c0 = i * 512
        maps.append({
            "dt_s": np.ascontiguousarray(dt_cm[c0:c0 + 512]),
            "dtu_s": np.ascontiguousarray(dtu_cm[c0:c0 + 512]),
            "bn_f": bn_f, "cn_f": cn_f,
            "alog_s": np.ascontiguousarray(alog[c0 * NST:(c0 + 512) * NST]),
        })
    r2 = _run(build_p2(), maps)

    y_cm = np.concatenate([np.asarray(r["y_s"]) for r in r2], axis=0)
    maps = []
    for i in range(NCORES):
        bsel, s0 = i // 4, (i % 4) * TPC
        maps.append({
            "y_t": np.ascontiguousarray(y_cm[:, i * TPC:(i + 1) * TPC]),
            "hs": np.asarray(r1[i]["hs"]), "gate": np.asarray(r1[i]["gate"]),
            "xT": np.ascontiguousarray(xtb[bsel][:, s0:s0 + TPC]),
            "w_out": ins["L0_attn_out_proj_w"], "d_par": ins["L0_attn_D_param"],
        })
    r3 = _run(build_p3(), maps)

    gidxT = np.ascontiguousarray(np.asarray(ins["L0_ffn_gate_idx"]).T).astype(np.int32)
    uidxT = np.ascontiguousarray(np.asarray(ins["L0_ffn_up_idx"]).T).astype(np.int32)
    didxT = np.ascontiguousarray(np.asarray(ins["L0_ffn_down_idx"]).T).astype(np.int32)
    RG = DM // NCORES
    RD = GH // NCORES
    maps = []
    for i in range(NCORES):
        maps.append({
            "gidxT": np.ascontiguousarray(gidxT[i * RG:(i + 1) * RG]),
            "uidxT": np.ascontiguousarray(uidxT[i * RG:(i + 1) * RG]),
            "didxT": np.ascontiguousarray(didxT[i * RD:(i + 1) * RD]),
            "lut": ins["L0_ffn_lut"],
        })
    r4 = _run(build_p4(), maps)
    wg_f = np.concatenate([np.asarray(r["wgt"]) for r in r4], axis=0)
    wu_f = np.concatenate([np.asarray(r["wut"]) for r in r4], axis=0)
    wd_f = np.concatenate([np.asarray(r["wdt"]) for r in r4], axis=0)

    maps = [{"x1": np.asarray(r3[i]["x1"]), "wg_f": wg_f, "wu_f": wu_f, "wd_f": wd_f}
            for i in range(NCORES)]
    r5 = _run(build_p5(), maps)

    maps = [{"x2": np.asarray(r5[i]["x2"]),
             "gam": ins["L1_attn_ln_gamma"], "bet": ins["L1_attn_ln_beta"],
             "w_in2": ins["L1_attn_W_in"], "b_in2": ins["L1_attn_b_in"]}
            for i in range(NCORES)]
    r6a = _run(build_p6a(), maps)

    u_full = [np.concatenate([np.asarray(r6a[b * 4 + k]["u"]) for k in range(4)], axis=1)
              for b in range(2)]
    maps = []
    for i in range(NCORES):
        bsel, s0 = i // 4, (i % 4) * TPC
        ush = np.zeros((8, 8192), np.float32)
        pre = u_full[bsel][:, 0:s0 + TPC]
        ush[:, 8192 - pre.shape[1]:] = pre
        maps.append({
            "u_sh": ush, "x2": np.asarray(r5[i]["x2"]), "xn": np.asarray(r6a[i]["xn"]),
            "dlog": ins["L1_attn_decay_logit"],
            "w_out2": ins["L1_attn_W_out"], "b_out2": ins["L1_attn_b_out"],
            "d_skip": ins["L1_attn_D_skip"],
            "gam": ins["L1_attn_ln_gamma"], "bet": ins["L1_attn_ln_beta"],
            "w_down": ins["L1_attn_W_down"], "b_down": ins["L1_attn_b_down"],
            "w_up": ins["L1_attn_W_up"], "b_up": ins["L1_attn_b_up"],
        })
    r6b = _run(build_p6b(), maps)

    maps = []
    for i in range(NCORES):
        mp = np.zeros((MOE_E, 1), np.float32)
        mp[:i] = 1.0
        maps.append({"x3": np.asarray(r6b[i]["x3"]), "wg8": ins["L1_ffn_Wg"],
                     "maskprev": mp, "tbase": np.array([i * TPC], np.float32)})
    r7 = _run(build_p7(), maps)
    flat = np.concatenate([np.asarray(r["flat_o"])[:, 0] for r in r7])
    gk = np.concatenate([np.asarray(r["gk_o"])[:, 0] for r in r7])
    inv = np.asarray(r7[0]["inv_o"])[:, 0]

    x3r = np.concatenate([np.ascontiguousarray(np.asarray(r6b[i]["x3"]).T)
                          for i in range(NCORES)], axis=0)
    maps = []
    for e in range(NCORES):
        maps.append({
            "x3r": x3r,
            "inv_s": np.ascontiguousarray(inv[e * MOE_C:(e + 1) * MOE_C].reshape(-1, 1)),
            "w1e": np.asarray(ins["L1_ffn_W1"][e]), "w3e": np.asarray(ins["L1_ffn_W3"][e]),
            "w2e": np.asarray(ins["L1_ffn_W2"][e]),
            "x3": np.asarray(r6b[e]["x3"]),
            "sw1": ins["L1_ffn_sw1"], "sw3": ins["L1_ffn_sw3"], "sw2": ins["L1_ffn_sw2"],
        })
    r8 = _run(build_p8(), maps)
    oe_f = np.concatenate([np.asarray(r["oe_r"]) for r in r8], axis=0)

    maps = []
    for i in range(NCORES):
        maps.append({
            "oe_f": oe_f,
            "flat_s": np.ascontiguousarray(flat[i * TPC:(i + 1) * TPC].reshape(-1, 1)).astype(np.int32),
            "gk_s": np.ascontiguousarray(gk[i * TPC:(i + 1) * TPC].reshape(-1, 1)),
            "sh_o": np.asarray(r8[i]["sh_o"]),
            "x3": np.asarray(r6b[i]["x3"]),
        })
    r9 = _run(build_p9(), maps)

    out = np.empty((2, 8192, DM), np.float32)
    for i in range(NCORES):
        bsel, s0 = i // 4, (i % 4) * TPC
        out[bsel, s0:s0 + TPC, :] = np.asarray(r9[i]["x4"]).T
    return out
